# version 18
# baseline (speedup 1.0000x reference)
import numpy as np
from contextlib import ExitStack

import concourse.bass as bass
import concourse.mybir as mybir
from concourse.bass_utils import run_bass_kernel_spmd

F32 = mybir.dt.float32
BF16 = mybir.dt.bfloat16
U8 = mybir.dt.uint8
AF = mybir.ActivationFunctionType
ALU = mybir.AluOpType

NCORE = 8
BLOC = 2
L = 2048
D = 1024
NCHUNK = 8
TC = 512
NTC = 4
NG = 8
NN = 16
EPS = 1e-5
W1_RING = 32
W1_BATCH = 16
DEBUG_DUMP = False
NO_BCAST = False


class Bld:
    def __init__(self, known):
        self.known = known
        self.events = {}
        names = ['pe', 'act', 'dve', 'gps', 'qs', 'qg', 'qr',
                 'qcv0a', 'qcv0b', 'qcv1a', 'qcv1b', 'qcv2a', 'qcv2b',
                 'qgl0', 'qgl1', 'qgl2', 'qgl3',
                 'qgs0', 'qgs1', 'qgs2', 'qgs3',
                 'qx0', 'qx1', 'qwin0', 'qwin1', 'qdt0', 'qdt1',
                 'qw00', 'qw01', 'qw10', 'qw11', 'qw12', 'qw13',
                 'qw20', 'qw21', 'qwo0', 'qwo1',
                 'qcu0', 'qcu1', 'qcu2', 'qcu3',
                 'qst0', 'qst1']
        self.counts = {k: 0 for k in names}
        self.sem_names = names
        self.sems = {}

    def inc(self, instr, sem, n=1, ev=None):
        self.counts[sem] += n
        if ev is not None:
            if isinstance(ev, str):
                ev = [ev]
            for e in ev:
                assert e not in self.events, f"duplicate event {e}"
                self.events[e] = (sem, self.counts[sem])
        instr.then_inc(self.sems[sem], n)

    def mark(self, sem, ev):
        if isinstance(ev, str):
            ev = [ev]
        for e in ev:
            assert e not in self.events, f"duplicate event {e}"
            self.events[e] = (sem, self.counts[sem])

    def wt(self, eng, ev):
        if not ev:
            return
        if self.known is not None:
            sem, val = self.known[ev]
            eng.wait_ge(self.sems[sem], val)
        elif ev in self.events:
            sem, val = self.events[ev]
            eng.wait_ge(self.sems[sem], val)

    def wts(self, eng, evs):
        for e in evs:
            self.wt(eng, e)


def ffn_schedule():
    items = []
    for o in range(32):
        items.append((4 + 2 * o, 'f0', o))
    for o in range(32):
        items.append((70 + (o * 18) // 10, 'f1', o))
    for n in range(4, 16):
        items.append((8 * n - 6, 'bc', n))
    items.sort(key=lambda t: (t[0], t[1]))
    return items


KLIST = ([('k0', i) for i in range(32)] +
         [('k1', i) for i in range(128)] +
         [('k2', i) for i in range(32)])


def kmpos(q):
    return 7 * (q // 6) + 1 + q % 6


def qgpos(g):
    return 7 * g


class EngWrap:

    def __init__(self, eng):
        self._eng = eng
        self.last = None

    def __getattr__(self, name):
        f = getattr(self._eng, name)
        if not callable(f):
            return f
        def wrap(*a, **kw):
            r = f(*a, **kw)
            if name not in ('wait_ge', 'then_inc'):
                self.last = r
            return r
        return wrap


def build_program(known):
    nc = bass.Bass("TRN2", target_bir_lowering=False,
                   detect_race_conditions=False)
    b = Bld(known)

    dt_ = nc.dram_tensor
    x_in = dt_("x_in", [BLOC, L, D], F32, kind="ExternalInput").ap()
    win_h = dt_("win_h", [128, 16, 8, 128], F32, kind="ExternalInput").ap()
    wdt_h = dt_("wdt_h", [128, 8, 8, 128], F32, kind="ExternalInput").ap()
    wout_h = dt_("wout_h", [128, 2, 8, 512], F32, kind="ExternalInput").ap()
    wbc_h = dt_("wbc_h", [128, 8, 32], F32, kind="ExternalInput").ap()
    k0_h = dt_("k0_h", [2, 8, 128, 4096], U8, kind="ExternalInput").ap()
    k1_h = dt_("k1_h", [32, 128, 4096], U8, kind="ExternalInput").ap()
    k2_h = dt_("k2_h", [32, 128, 1024], U8, kind="ExternalInput").ap()
    cst_h = dt_("cst_h", [128, 204], F32, kind="ExternalInput").ap()
    outb_h = dt_("outb_h", [128, 1024], F32, kind="ExternalInput").ap()
    lng_h = dt_("lng_h", [128, 1024], F32, kind="ExternalInput").ap()
    lnb_h = dt_("lnb_h", [128, 1024], F32, kind="ExternalInput").ap()
    out_hbm = dt_("out", [BLOC, L, D], F32, kind="ExternalOutput").ap()

    dbg = {}
    if DEBUG_DUMP:
        for nm, shp, dty in [("d_xcb", [128, 8, 512], BF16),
                             ("d_lbf", [128, 8, 512], BF16),
                             ("d_gate", [128, 8, 512], BF16),
                             ("d_fy", [128, 8, 512], BF16),
                             ("d_yacc", [128, 8, 512], BF16),
                             ("d_h1", [128, 32, 512], BF16),
                             ("d_h2", [128, 32, 512], BF16),
                             ("d_dtx", [128, 8, 512], BF16),
                             ("d_zb", [128, 2, 1024], F32),
                             ("d_negb", [128, 4, 512], BF16),
                             ("d_cbr", [128, 4, 512], BF16),
                             ("d_stats", [128, 10], F32),
                             ("d_outbr", [128, 1024], BF16),
                             ("d_lngr", [128, 1024], BF16),
                             ("d_lnbr", [128, 1024], BF16),
                             ("d_base", [128, 8, 512], BF16),
                             ("d_rout", [128, 4, 512], BF16),
                             ("d_hc", [128, 128], F32),
                             ("d_xst", [128, 2, 1024], F32),
                             ("d_pm", [128, 8, 512], F32),
                             ("d_zall", [128, 4, 1024], F32),
                             ("d_ball", [128, 8, 512], F32),
                             ("zlog", [NCHUNK, 4, 128, 1024], F32),
                             ("blog", [NCHUNK, 8, 128, 512], F32),
                             ("d_rw1", [128, 32, 128], BF16),
                             ("d_rw0", [128, 2, 8, 128], BF16),
                             ("d_mu", [128, 4], F32),
                             ("d_rs", [128, 4], F32)]:
            dbg[nm] = dt_(nm, shp, dty, kind="ExternalOutput").ap()
    win_s = dt_("win_s", [16, 128, 1024], BF16).ap()
    wdt_s = dt_("wdt_s", [8, 128, 1024], BF16).ap()
    wout_s = dt_("wout_s", [2, 128, 4096], BF16).ap()
    w0_s = dt_("w0_s", [32, 128, 1024], BF16).ap()
    w1_s = dt_("w1_s", [32, 128, 4096], BF16).ap()
    w2_s = dt_("w2_s", [2, 128, 16384], BF16).ap()

    with ExitStack() as st:
        sb = lambda name, shape, dt: st.enter_context(
            nc.sbuf_tensor(name, shape, dt))
        ps = lambda name, shape, dt: st.enter_context(
            nc.psum_tensor(name, shape, dt))
        for s_ in b.sem_names:
            b.sems[s_] = st.enter_context(nc.semaphore(s_))

        ident = sb("ident", [128, 128], F32)
        apos = sb("apos", [128, 128], F32)
        cst = sb("cst", [128, 204], F32)
        bcsc = sb("bcsc", [32, 1], F32)
        outbr = sb("outbr", [128, 1024], BF16)
        lngr = sb("lngr", [128, 1024], BF16)
        lnbr = sb("lnbr", [128, 1024], BF16)
        scol = sb("scol", [128, 3], F32)
        stats = sb("stats", [128, 10], F32)
        mu_c = sb("mu_c", [128, 4], F32)
        rs_c = sb("rs_c", [128, 4], F32)
        rin = sb("rin", [128, 2, 8, 128], BF16)
        rdt = sb("rdt", [128, 2, 8, 128], BF16)
        rout = sb("rout", [128, 4, 512], BF16)
        rw0 = sb("rw0", [128, 2, 8, 128], BF16)
        rw1 = sb("rw1", [128, W1_RING, 128], BF16)
        rw2 = sb("rw2", [128, 4, 512], BF16)
        rbc = sb("rbc", [128, 8, 32], BF16)
        xcb = sb("xcb", [128, 8, 512], BF16)
        gateb = sb("gateb", [128, 8, 512], BF16)
        lbf = sb("lbf", [128, 8, 512], BF16)
        dtxb = sb("dtxb", [128, 8, 512], BF16)
        yacc = sb("yacc", [128, 8, 512], BF16)
        fybuf = sb("fybuf", [128, 8, 512], BF16)
        hc = sb("hc", [128, 128], F32)
        sdt = sb("sdt", [128, 512], F32)
        convacc = sb("convacc", [128, 2, 512], F32)
        xssm = sb("xssm", [128, 2, 515], F32)
        pads = sb("pads", [128, 8, 3], F32)
        negb = sb("negb", [128, 3, 512], BF16)
        cbr = sb("cbr", [128, 3, 512], BF16)
        da = sb("da", [128, 2, 512], F32)
        ub = sb("ub", [128, 512], BF16)
        hb = sb("hb", [128, 2, 512], F32)
        pb = sb("pb", [128, 2, 512], F32)
        h1 = sb("h1", [128, 32, 512], BF16)
        h2 = sb("h2", [128, 32, 512], BF16)
        xst = sb("xst", [128, 2, 1024], F32)
        zb = sb("zb", [128, 2, 1024], F32)
        baseb = sb("baseb", [128, 8, 512], BF16)
        bcst = sb("bcst", [32, 2, 512], BF16)
        mrhs = sb("mrhs", [32, 2, 512], BF16)
        ones32 = sb("ones32", [32, 128], BF16)
        cu8 = baseb[:, 0:4, :].rearrange("p a t -> p (a t)") \
            .bitcast(U8).rearrange("p (a t) -> p a t", a=4)
        cbfA = rw2[:, :, :].rearrange("p a t -> p (a t)") \
            .rearrange("p (a b) -> p a b", a=2)
        cbfB = rout[:, :, :].rearrange("p a t -> p (a t)") \
            .rearrange("p (a b) -> p a b", a=2)
        cbf_slot = lambda s: (cbfA if s < 2 else cbfB)[:, s % 2, :]
        cb2 = baseb[:, 4:8, :].rearrange("p a t -> p (a t)") \
            .rearrange("p (a b) -> p a b", a=2)

        OB, OCW, OCB, ONDT, OBC, OD, OS, OA = 0, 16, 48, 56, 64, 65, 73, 76
        pm = [ps(f"pm{i}", [128, 512], F32) for i in range(8)]
        bank_w = [None] * 8
        bank_r = [None] * 8

        with nc.Block() as block:

            @block.sync
            def _(sync):
                def dma(dst, src, ev=None, wait=(), q='qs'):
                    if isinstance(wait, str):
                        wait = (wait,)
                    b.wts(sync, [w for w in wait if w])
                    b.inc(sync.dma_start(out=dst, in_=src), q, 16, ev=ev)

                dma(cst[:, :], cst_h, ev='ld_small')
                dma(sdt[:, 0:256], wbc_h.rearrange("p g m -> p (g m)"),
                    ev='ld_wbc', wait='ld_small')

                dma(xst[:, 0, :], x_in[0, 0:128, :], ev='xB:0:0', q='qx0')
                dma(xst[:, 1, :], x_in[0, 128:256, :], ev='xB:0:1', q='qx1')

                def w1_load(c, bi):
                    t0 = bi * W1_BATCH
                    o_chunk, kk = divmod(t0, 32)
                    w = []
                    if bi >= W1_RING // W1_BATCH:
                        w.append(f'w1b:{c}:{bi - W1_RING // W1_BATCH}')
                    elif c > 0:
                        w.append(f'w1b:{c-1}:{bi + 64 - W1_RING // W1_BATCH}')
                    if c == 0:
                        w += [f'cvS:k1:{(o_chunk // 8) * 32 + t}'
                              for t in (30, 31)]
                    sl = t0 % W1_RING
                    dma(rw1[:, sl:sl + W1_BATCH, :],
                        w1_s[o_chunk, :,
                             kk * 128:(kk + W1_BATCH) * 128]
                        .rearrange("p (k m) -> p k m", k=W1_BATCH),
                        ev=f'w1L:{c}:{bi}', wait=w, q=f'qw1{bi % 4}')

                def kload(q):
                    fam, i = KLIST[q]
                    if fam == 'k0':
                        j, k = divmod(i, 8)
                        for half in (0, 1):
                            dma(cu8[:, (2 * i + half) % 4, :],
                                k0_h[half, k, :, j * 1024:(j + 1) * 1024],
                                ev=f'cvL{half}:k0:{i}',
                                wait=(f'cvD:k0:{i-2}',) if i >= 2 else (),
                                q=f'qcu{(2 * i + half) % 4}')
                    elif fam == 'k1':
                        j, k = divmod(i, 32)
                        dma(cu8[:, i % 4, :],
                            k1_h[k, :, j * 1024:(j + 1) * 1024],
                            ev=f'cvL:k1:{i}',
                            wait=(f'cvD:k1:{i-4}',) if i >= 4
                            else (f'cvD:k0:{30 + i // 2}',),
                            q=f'qcu{i % 4}')
                    else:
                        dma(cu8[:, i % 4, :], k2_h[i, :, :],
                            ev=f'cvL:k2:{i}',
                            wait=(f'cvD:k2:{i-4}',) if i >= 4
                            else (f'cvD:k1:{124 + i}',),
                            q=f'qcu{i % 4}')

                def kstore(q):
                    fam, i = KLIST[q]
                    b.wt(sync, f'cvD:{fam}:{i}')
                    if fam == 'k0':
                        j, k = divmod(i, 8)
                        out = w0_s[j * 8:(j + 1) * 8, :,
                                   k * 128:(k + 1) * 128] \
                            .rearrange("o p m -> p o m")
                        in_ = cb2[:, q % 2, :].rearrange("p (o m) -> p o m",
                                                         o=8)
                        sem = 'qcv0' + 'ab'[i % 2]
                    elif fam == 'k1':
                        j, k = divmod(i, 32)
                        out = w1_s[j * 8:(j + 1) * 8, :,
                                   k * 128:(k + 1) * 128] \
                            .rearrange("o p m -> p o m")
                        in_ = cb2[:, q % 2, :].rearrange("p (o m) -> p o m",
                                                         o=8)
                        sem = 'qcv1' + 'ab'[i % 2]
                    else:
                        out = w2_s[:, :, i * 512:(i + 1) * 512] \
                            .rearrange("h p o -> p h o")
                        in_ = cb2[:, q % 2, :].rearrange("p (h o) -> p h o",
                                                         h=2)
                        sem = 'qcv2' + 'ab'[i % 2]
                    b.inc(sync.dma_start(out=out, in_=in_), sem, 16,
                          ev=f'cvS:{fam}:{i}')

                def chunk_zipper(c):
                    s, j0 = c // 4, (c % 4)
                    tok0 = j0 * TC
                    items = []
                    if c > 0:
                        for tc in (0, 1):
                            items.append((-300 + tc, lambda tc=tc: dma(
                                xst[:, tc, :],
                                x_in[s, tok0 + tc * 128:tok0 + (tc + 1) * 128, :],
                                ev=f'xB:{c}:{tc}',
                                wait=(f'xEc:{c-1}:{tc+2}',), q=f'qx{tc}')))
                        for tc in (2, 3):
                            items.append((-298 + tc, lambda tc=tc: dma(
                                xst[:, tc % 2, :],
                                x_in[s, tok0 + tc * 128:tok0 + (tc + 1) * 128, :],
                                ev=f'xB:{c}:{tc}',
                                wait=(f'xBc:{c}:{tc-2}',), q=f'qx{tc % 2}')))
                    else:
                        for tc in (2, 3):
                            items.append((-227.5 + 0.01 * tc, lambda tc=tc: dma(
                                xst[:, tc % 2, :],
                                x_in[0, tc * 128:(tc + 1) * 128, :],
                                ev=f'xB:0:{tc}', wait=(f'xBc:0:{tc-2}',),
                                q=f'qx{tc % 2}')))
                    post = lambda k: -90.0 + (k + 300.0) * 0.01
                    if c == 0:
                        for q in range(len(KLIST)):
                            if q < 32:
                                lk = -250 + 0.7 * max(q - 3, 0)
                                sk = -250 + 0.7 * q + 0.35
                            else:
                                lk = -227.0 + 0.55 * (max(q - 3, 32) - 32)
                                sk = -227.0 + 0.55 * (q - 32) + 0.3
                            items.append((lk, lambda q=q: kload(q)))
                            items.append((sk, lambda q=q: kstore(q)))
                    for o in range(16):
                        items.append(((-249 + 0.01 * o if o < 2
                                       else -227.4 + 0.005 * o) if c == 0
                                      else -200 + o,
                                      lambda o=o: ring_win1(c, o)))
                    for o in range(8):
                        items.append((-227.3 + 0.002 * o if c == 0
                                      else -100 + o,
                                      lambda o=o: ring_dt1(c, o)))
                    for o in range(32):
                        kk0 = -226.95 + 0.002 * o
                        items.append((kk0 if c == 0
                                      else (-10 + 2 * o if o < 2 else 2 * o),
                                      lambda o=o: ring_w01(c, o)))
                    for bi in range(64):
                        kk = 30 if bi < 2 else 64 + (bi - 2) * 1.8
                        items.append((post(kk) if c == 0 else kk,
                                      lambda bi=bi: w1_load(c, bi)))
                    for bi in range(32):
                        items.append((post(300 + bi) if c == 0 else 300 + bi,
                                      lambda bi=bi: ring_w21(c, bi)))
                    for bi in range(8):
                        items.append((post(400 + bi) if c == 0 else 400 + bi,
                                      lambda bi=bi: ring_wout1(c, bi)))
                    for tc in range(NTC):
                        items.append((post(500 + tc) if c == 0 else 500 + tc,
                                      lambda tc=tc: dma(
                            xst[:, tc % 2, :],
                            x_in[s, tok0 + tc * 128:tok0 + (tc + 1) * 128, :],
                            ev=f'xE:{c}:{tc}',
                            wait=[f'xBc:{c}:{2 + (tc % 2)}'] +
                            ([f'xEc:{c}:{tc-2}'] if tc >= 2 else []),
                            q=f'qx{tc % 2}')))
                    items.sort(key=lambda t: t[0])
                    for _, f in items:
                        f()

                def ring_win1(c, o):
                    w = []
                    if o >= 2:
                        w.append(f'winoc:{c}:{o-2}')
                    elif c > 0:
                        w.append(f'winoc:{c-1}:{o+14}')
                    if c == 0:
                        w.append(f'cvS:win:{o}')
                    dma(rin[:, o % 2, :, :],
                        win_s[o, :, :].rearrange("p (k m) -> p k m", k=8),
                        ev=f'winL:{c}:{o}', wait=w, q=f'qwin{o % 2}')

                def ring_dt1(c, o):
                    w = []
                    if o >= 2:
                        w.append(f'dtoc:{c}:{o-2}')
                    elif c > 0:
                        w.append(f'dtoc:{c-1}:{o+6}')
                    if c == 0:
                        w.append(f'cvS:wdt:{o}')
                    dma(rdt[:, o % 2, :, :],
                        wdt_s[o, :, :].rearrange("p (k m) -> p k m", k=8),
                        ev=f'dtL:{c}:{o}', wait=w, q=f'qdt{o % 2}')

                def ring_w01(c, o):
                    w = []
                    if o >= 2:
                        w.append(f'w0oc:{c}:{o-2}')
                    elif c > 0:
                        w.append(f'w0oc:{c-1}:{o+30}')
                    if c == 0:
                        w += [f'cvS:k0:{(o // 8) * 8 + t}' for t in (6, 7)]
                    dma(rw0[:, o % 2, :, :],
                        w0_s[o, :, :].rearrange("p (k m) -> p k m", k=8),
                        ev=f'w0L:{c}:{o}', wait=w, q=f'qw0{o % 2}')

                def ring_w21(c, bi):
                    h_, k_ = divmod(bi * 2, 32)
                    w = []
                    if bi >= 2:
                        w.append(f'w2b:{c}:{bi-2}')
                    elif c > 0:
                        w.append(f'w2b:{c-1}:{bi+30}')
                    if c == 0:
                        w.append(f'cvS:k2:{(k_ // 2) * 2}')
                        w.append(f'cvS:k2:{(k_ // 2) * 2 + 1}')
                        if bi < 2:
                            w.append(f'cvS:wout:{4 + bi}')
                    sl = (bi * 2) % 4
                    dma(rw2[:, sl:sl + 2, :],
                        w2_s[h_, :, k_ * 512:(k_ + 2) * 512]
                        .rearrange("p (k o) -> p k o", k=2),
                        ev=f'w2L:{c}:{bi}', wait=w, q=f'qw2{bi % 2}')

                def ring_wout1(c, bi):
                    h_, k_ = divmod(bi * 2, 8)
                    w = []
                    if bi >= 2:
                        w.append(f'wob:{c}:{bi-2}')
                    elif c > 0:
                        w.append(f'wob:{c-1}:{bi+6}')
                    if c == 0:
                        w.append(f'cvS:wout:{bi}')
                        if bi < 2:
                            w.append(f'cvS:wout:{6 + bi}')
                    sl = (bi * 2) % 4
                    dma(rout[:, sl:sl + 2, :],
                        wout_s[h_, :, k_ * 512:(k_ + 2) * 512]
                        .rearrange("p (k o) -> p k o", k=2),
                        ev=f'woL:{c}:{bi}', wait=w, q=f'qwo{bi % 2}')

                for c in range(NCHUNK):
                    chunk_zipper(c)

            @block.tensor
            def _(tensor):
                def mm(bank, lhsT, rhs, start, stop, ev=None, wait=()):
                    b.wts(tensor, wait)
                    ins = tensor.matmul(pm[bank][0:lhsT.shape[-1], 0:rhs.shape[-1]],
                                        lhsT, rhs, start=start, stop=stop)
                    if ev:
                        b.inc(ins, 'pe', 1, ev=ev)

                def bcmm(c, n):
                    bkB = 4 + (n % 2) * 2
                    for kind, bk in ((0, bkB), (1, bkB + 1)):
                        w = [f'mrhs{"B" if kind == 0 else "C"}:{c}:{n}']
                        if c == 0 and n == 0 and kind == 0:
                            w.append('ones32R')
                        if bank_r[bk] is not None:
                            w.append(bank_r[bk])
                        mm(bk, ones32[:, :],
                           mrhs[:, kind, :], True, True,
                           ev=f'bcmm{"B" if kind == 0 else "C"}:{c}:{n}',
                           wait=w)
                        bank_w[bk] = f'bcmm{"B" if kind == 0 else "C"}:{c}:{n}'
                        bank_r[bk] = f'bcev:{c}:{n}'

                for c in range(NCHUNK):
                    for tc in range(NTC):
                        bk = tc % 2
                        waits = [f'xn32:{c}:{tc}']
                        for bb in (bk, bk + 2):
                            if bank_r[bb] is not None:
                                waits.append(bank_r[bb])
                        if c == 0 and tc == 0:
                            waits.append('identR')
                        for h in range(8):
                            b.wts(tensor, waits)
                            waits = []
                            ins = tensor.transpose(
                                pm[bk + (0 if h < 4 else 2)]
                                [:, (h % 4) * 128:(h % 4 + 1) * 128],
                                zb[:, tc % 2, h * 128:(h + 1) * 128],
                                ident[:, :])
                            if h == 7:
                                b.inc(ins, 'pe', 1, ev=f'tpg:{c}:{tc}')
                        bank_w[bk] = bank_w[bk + 2] = f'tpg:{c}:{tc}'
                        bank_r[bk] = bank_r[bk + 2] = f'tpe:{c}:{tc}'

                    for g in range(NG):
                        bk = g % 4
                        waits = [f'winL:{c}:{g}']
                        if g == 0:
                            waits.append(f'tpe:{c}:3')
                        if bank_r[bk] is not None:
                            waits.append(bank_r[bk])
                        for k in range(8):
                            mm(bk, rin[:, g % 2, k, :], fybuf[:, k, :],
                               k == 0, k == 7,
                               ev=[f'ipg:{c}:{g}', f'winoc:{c}:{g}'] if k == 7 else None,
                               wait=waits if k == 0 else ())
                            waits = []
                        bank_w[bk] = f'ipg:{c}:{g}'
                        bank_r[bk] = f'ipe:{c}:{g}'
                    for g in range(NG):
                        o = 8 + g
                        bk = g % 4
                        waits = [f'winL:{c}:{o}']
                        if bank_r[bk] is not None:
                            waits.append(bank_r[bk])
                        for k in range(8):
                            mm(bk, rin[:, o % 2, k, :], fybuf[:, k, :],
                               k == 0, k == 7,
                               ev=[f'gpg:{c}:{g}', f'winoc:{c}:{o}'] if k == 7 else None,
                               wait=waits if k == 0 else ())
                            waits = []
                        bank_w[bk] = f'gpg:{c}:{g}'
                        bank_r[bk] = f'gpe:{c}:{g}'
                    bk = 0
                    waits = [f'xcbd:{c}:{NG-1}']
                    if c == 0:
                        waits.append('ld_rbc')
                    if bank_r[bk] is not None:
                        waits.append(bank_r[bk])
                    for k in range(8):
                        mm(bk, rbc[:, k, :], xcb[:, k, :], k == 0, k == 7,
                           ev=f'bcg:{c}' if k == 7 else None,
                           wait=waits if k == 0 else ())
                        waits = []
                    bank_w[0] = f'bcg:{c}'
                    bank_r[0] = f'bce:{c}'
                    for g in range(NG):
                        bk = 1 + g % 3
                        waits = [f'dtL:{c}:{g}']
                        if bank_r[bk] is not None:
                            waits.append(bank_r[bk])
                        for k in range(8):
                            mm(bk, rdt[:, g % 2, k, :], xcb[:, k, :],
                               k == 0, k == 7,
                               ev=[f'dtg:{c}:{g}', f'dtoc:{c}:{g}'] if k == 7 else None,
                               wait=waits if k == 0 else ())
                            waits = []
                        bank_w[bk] = f'dtg:{c}:{g}'
                        bank_r[bk] = f'dte:{c}:{g}'
                    def pe_f0(o):
                        bk = o % 4
                        waits = [f'w0L:{c}:{o}']
                        if bank_r[bk] is not None:
                            waits.append(bank_r[bk])
                        for k in range(8):
                            mm(bk, rw0[:, o % 2, k, :], fybuf[:, k, :],
                               k == 0, k == 7,
                               ev=[f'f0g:{c}:{o}', f'w0oc:{c}:{o}']
                               if k == 7 else None,
                               wait=waits if k == 0 else ())
                            waits = []
                        bank_w[bk] = f'f0g:{c}:{o}'
                        bank_r[bk] = f'f0e:{c}:{o}'
                        if o == 31:
                            b.mark('pe', f'ffn0done:{c}')

                    def pe_f1(o):
                        bk = o % 4
                        for k in range(32):
                            ti = o * 32 + k
                            bi = ti // W1_BATCH
                            w = [f'h1e:{c}:{k}'] if o == 0 else []
                            if ti % W1_BATCH == 0:
                                w.append(f'w1L:{c}:{bi}')
                            if k == 0 and bank_r[bk] is not None:
                                w.append(bank_r[bk])
                            ev = []
                            if k == 31:
                                ev.append(f'f1g:{c}:{o}')
                            if (ti + 1) % W1_BATCH == 0:
                                ev.append(f'w1b:{c}:{bi}')
                            mm(bk, rw1[:, ti % W1_RING, :], h1[:, k, :],
                               k == 0, k == 31, ev=ev or None, wait=w)
                        bank_w[bk] = f'f1g:{c}:{o}'
                        bank_r[bk] = f'f1e:{c}:{o}'

                    if c == 0:
                        for o in range(32):
                            pe_f0(o)
                        for o in range(32):
                            pe_f1(o)
                        b.mark('pe', f'ffn1done:{c}')
                        for n_ in range(16):
                            bcmm(c, n_)
                    else:
                        for n_ in range(4):
                            bcmm(c, n_)
                        for _pos, kind_, idx_ in ffn_schedule():
                            if kind_ == 'bc':
                                bcmm(c, idx_)
                            elif kind_ == 'f0':
                                pe_f0(idx_)
                            else:
                                pe_f1(idx_)
                        b.mark('pe', f'ffn1done:{c}')
                    for h in range(2):
                        for k in range(32):
                            hk = h * 32 + k
                            for tc in range(NTC):
                                bk = 4 + tc
                                w = []
                                if tc == 0:
                                    if h == 0:
                                        w.append(f'h2e:{c}:{k}')
                                    if hk % 2 == 0:
                                        w.append(f'w2L:{c}:{hk // 2}')
                                if k == 0 and bank_r[bk] is not None:
                                    w.append(bank_r[bk])
                                ev = []
                                if k == 31:
                                    ev.append(f'f2g:{c}:{h}:{tc}')
                                if tc == 3 and hk % 2 == 1:
                                    ev.append(f'w2b:{c}:{hk // 2}')
                                mm(bk, h2[:, k, tc * 128:(tc + 1) * 128],
                                   rw2[:, hk % 4, :], k == 0, k == 31,
                                   ev=ev or None, wait=w)
                        for tc in range(NTC):
                            bank_w[4 + tc] = f'f2g:{c}:{h}:{tc}'
                            bank_r[4 + tc] = f'f2e:{c}:{h}:{tc}'
                    b.mark('pe', f'ffn2done:{c}')
                    for h in range(2):
                        for k in range(8):
                            hk = h * 8 + k
                            for tc in range(NTC):
                                bk = tc * 2 + h
                                w = []
                                if tc == 0:
                                    if hk == 0:
                                        w.append(f'ybfall:{c}')
                                    if hk % 2 == 0:
                                        w.append(f'woL:{c}:{hk // 2}')
                                if k == 0 and bank_r[bk] is not None:
                                    w.append(bank_r[bk])
                                ev = []
                                if k == 7:
                                    ev.append(f'opg:{c}:{tc}:{h}')
                                if tc == 3 and hk % 2 == 1:
                                    ev.append(f'wob:{c}:{hk // 2}')
                                mm(bk, fybuf[:, k, tc * 128:(tc + 1) * 128],
                                   rout[:, hk % 4, :], k == 0, k == 7,
                                   ev=ev or None, wait=w)
                    for h in range(2):
                        for tc in range(NTC):
                            bank_w[tc * 2 + h] = f'opg:{c}:{tc}:{h}'
                            bank_r[tc * 2 + h] = f'ope:{c}:{tc}'

            @block.scalar
            def _(scalar):
                def act(out, in_, func, ev=None, wait=(), **kw):
                    b.wts(scalar, wait)
                    ins = scalar.activation(out=out, in_=in_, func=func, **kw)
                    if ev:
                        b.inc(ins, 'act', 1, ev=ev)
                    return ins

                def afence(last_ins):
                    b.inc(last_ins, 'act', 1)
                    scalar.wait_ge(b.sems['act'], b.counts['act'])

                act(apos[:, :], cst[:, OA:OA + 128], AF.Exp, ev='aposR',
                    wait=('ld_small',))

                for c in range(NCHUNK):
                    for tc in range(NTC):
                        w = [f'xB:{c}:{tc}']
                        if tc >= 1:
                            w.append(f'ln1v:{c}:{tc-1}')
                        if tc >= 2:
                            w.append(f'tpg:{c}:{tc-2}')
                        elif c > 0:
                            w.append(f'mst:{c-1}:{tc+2}')
                        act(zb[:, tc % 2, :], xst[:, tc % 2, :], AF.Copy,
                            wait=w, accum_out=stats[:, 0:1])
                        act(zb[:, tc % 2, :], xst[:, tc % 2, :], AF.Square,
                            accum_out=stats[:, 1:2], ev=f'ln1s:{c}:{tc}')
                        ins = act(stats[:, 2:3], stats[:, 2:3], AF.Ln,
                                  wait=(f'ln1v:{c}:{tc}',))
                        afence(ins)
                        act(rs_c[:, tc:tc + 1], stats[:, 2:3], AF.Exp,
                            scale=-0.5, ev=f'ln1r:{c}:{tc}')
                    for g in range(NG):
                        w = [f'ipg:{c}:{g}', f'padc:{c}:{g}']
                        if g >= 2:
                            w.append(f'convd:{c}:{g-2}')
                        elif c > 0:
                            w.append(f'convd:{c-1}:{g+6}')
                        act(xssm[:, g % 2, 3:515], pm[g % 4][:, :], AF.Identity,
                            bias=cst[:, OB + g:OB + g + 1], wait=w, ev=f'ipe:{c}:{g}')
                        w = [f'conv4:{c}:{g}']
                        if c > 0:
                            w += [f'dtg:{c-1}:{NG-1}', f'dtxd:{c-1}:{g}']
                        act(xcb[:, g, :], convacc[:, g % 2, :], AF.Silu,
                            bias=cst[:, OCB + g:OCB + g + 1], wait=w, ev=f'xcbd:{c}:{g}')
                    for g in range(NG):
                        w = [f'gpg:{c}:{g}']
                        if c > 0:
                            w.append(f'ybf:{c-1}:{g}')
                        act(gateb[:, g, :], pm[g % 4][:, :], AF.Silu,
                            bias=cst[:, OB + 8 + g:OB + 9 + g], wait=w, ev=f'gpe:{c}:{g}')
                    w = [f'bcg:{c}']
                    if c == 0:
                        w.append('bccolR')
                    if c >= 2:
                        w.append(f'mrhsd:{c-2}:{2*NN-1}')
                    act(bcst[:, c % 2, :], pm[0][0:32, :], AF.Identity,
                        scale=bcsc[0:32, :], bias=cst[0:32, OBC:OBC + 1],
                        wait=w, ev=f'bce:{c}')
                    for g in range(NG):
                        w = [f'dtg:{c}:{g}']
                        if c == 0 and g == 0:
                            w.append('ndtbR')
                        act(sdt[:, :], pm[1 + g % 3][:, :], AF.Sigmoid,
                            scale=-1.0, bias=cst[:, ONDT + g:ONDT + g + 1], wait=w)
                        w = []
                        if c > 0:
                            w.append(f'dtxd:{c-1}:{g}')
                        act(lbf[:, g, :], sdt[:, :], AF.Ln, wait=w,
                            ev=[f'dte:{c}:{g}', f'lbfd:{c}:{g}'])
                    def bc_evac(n):
                        w = [f'bcmmC:{c}:{n}']
                        if n >= 3:
                            w += [f'bcuseB:{c}:{n-3}', f'bcuseC:{c}:{n-3}']
                        elif c > 0:
                            prev = 15 if n == 0 else 12 + n
                            w += [f'bcuseB:{c-1}:{prev}',
                                  f'bcuseC:{c-1}:{prev}']
                        bk = 4 + (n % 2) * 2
                        act(negb[:, n % 3, :], pm[bk][:, :], AF.Identity,
                            wait=w)
                        act(cbr[:, n % 3, :], pm[bk + 1][:, :], AF.Identity,
                            ev=[f'bcL:{c}:{n}', f'bcev:{c}:{n}'])

                    def f0_evac(o):
                        w = [f'f0g:{c}:{o}']
                        if c == 0 and o == 0:
                            w.append('scolR')
                        if c > 0:
                            w.append(f'ffn1done:{c-1}')
                        act(h1[:, o, :], pm[o % 4][:, :], AF.Relu,
                            scale=scol[:, 0:1], wait=w,
                            ev=[f'f0e:{c}:{o}', f'h1e:{c}:{o}'])

                    def f1_evac(o):
                        w = [f'f1g:{c}:{o}']
                        if c > 0:
                            w.append(f'ffn2done:{c-1}')
                        act(h2[:, o, :], pm[o % 4][:, :], AF.Relu,
                            scale=scol[:, 1:2], wait=w,
                            ev=[f'f1e:{c}:{o}', f'h2e:{c}:{o}'])

                    if c == 0:
                        f0pos, f1pos = {}, {}
                        for o in range(32):
                            f0_evac(o)
                        for o in range(32):
                            f1_evac(o)
                    else:
                        f0pos = {4 + o * 2: o for o in range(32)}
                        f1pos = {70 + (o * 18) // 10: o for o in range(32)}
                    bc_evac(0)
                    bc_evac(1)
                    bc_evac(2)
                    bcpos = {max(1, 8 * n - 6): n for n in range(3, NN)}
                    for i in range(128):
                        n, g = divmod(i, NG)
                        w = []
                        if c == 0 and i == 0:
                            w.append('aposR')
                        if n == 0:
                            w.append(f'lbfd:{c}:{g}')
                        if i >= 2:
                            w.append(f'scank:{c}:{i-2}')
                        elif c > 0:
                            w.append(f'scank:{c-1}:{126+i}')
                        act(da[:, i % 2, :], lbf[:, g, :], AF.Exp,
                            scale=apos[:, g * 16 + n:g * 16 + n + 1],
                            wait=w, ev=f'dA:{c}:{i}')
                        if i in bcpos:
                            bc_evac(bcpos[i])
                        if i in f0pos:
                            f0_evac(f0pos[i])
                        if i in f1pos:
                            f1_evac(f1pos[i])
                    for h in range(2):
                        for tc in range(NTC):
                            w = [f'f2g:{c}:{h}:{tc}']
                            if h == 0 and c > 0:
                                w.append(f'mcp:{c-1}:{tc}')
                            act(baseb[:, tc * 2 + h, :], pm[4 + tc][:, :],
                                AF.Copy, scale=scol[:, 2:3], wait=w,
                                ev=f'f2e:{c}:{h}:{tc}')
                    for tc in range(NTC):
                        w = [f'zrdy:{c}:{tc}']
                        if tc >= 1:
                            w.append(f'ln2v:{c}:{tc-1}')
                        act(sdt[:, :], zb[:, tc % 2, 0:512], AF.Copy,
                            accum_out=stats[:, 3:4], wait=w)
                        act(sdt[:, :], zb[:, tc % 2, 512:1024], AF.Copy,
                            accum_out=stats[:, 4:5])
                        act(sdt[:, :], zb[:, tc % 2, 0:512], AF.Square,
                            accum_out=stats[:, 5:6])
                        act(sdt[:, :], zb[:, tc % 2, 512:1024], AF.Square,
                            accum_out=stats[:, 6:7], ev=f'ln2s:{c}:{tc}')
                        ins = act(stats[:, 8:9], stats[:, 7:8], AF.Ln,
                                  wait=(f'ln2v:{c}:{tc}',))
                        afence(ins)
                        act(stats[:, 9:10], stats[:, 8:9], AF.Exp, scale=-0.5,
                            ev=f'ln2r:{c}:{tc}')

            @block.vector
            def _(vector):
                vector = EngWrap(vector)

                def w_(*evs):
                    b.wts(vector, [e for e in evs if e])

                def tick(ev):
                    b.inc(vector.last, 'dve', 1, ev=ev)

                def fence():
                    b.inc(vector.last, 'dve', 1)
                    vector.wait_ge(b.sems['dve'], b.counts['dve'])

                w_('ld_small')
                vector.tensor_scalar(out=cst[:, ONDT:ONDT + 8],
                                     in0=cst[:, ONDT:ONDT + 8],
                                     scalar1=-1.0, scalar2=None, op0=ALU.mult)
                tick('ndtbR')
                vector.tensor_scalar(out=scol[:, :], in0=cst[:, OS:OS + 3],
                                     scalar1=2.0, scalar2=None, op0=ALU.mult)
                tick('scolR')
                vector.memset(bcsc[0:32, :], 1.0)
                fence()
                vector.tensor_scalar(out=bcsc[0:16, :], in0=bcsc[0:16, :],
                                     scalar1=-1.0, scalar2=None, op0=ALU.mult)
                vector.tensor_scalar(out=cst[0:16, OBC:OBC + 1],
                                     in0=cst[0:16, OBC:OBC + 1],
                                     scalar1=-1.0, scalar2=None, op0=ALU.mult)
                tick('bccolR')
                w_('ld_wbc')
                vector.tensor_copy(out=rbc[:, :, :],
                                   in_=sdt[:, 0:256]
                                   .rearrange("p (g m) -> p g m", g=8))
                tick('ld_rbc')

                def kconv_range(q0, q1):
                    for q in range(q0, q1):
                        fam, i = KLIST[q]
                        ws = []
                        if q >= 2:
                            pf, pi = KLIST[q - 2]
                            ws.append(f'cvS:{pf}:{pi}')
                        if fam == 'k0':
                            ws += [f'cvL0:k0:{i}', f'cvL1:k0:{i}']
                            w_(*ws)
                            vector.tensor_tensor(
                                out=cb2[:, q % 2, :],
                                in0=cu8[:, (2 * i) % 4, :],
                                in1=cu8[:, (2 * i + 1) % 4, :],
                                op=ALU.subtract)
                        else:
                            ws.append(f'cvL:{fam}:{i}')
                            w_(*ws)
                            vector.tensor_scalar(out=cb2[:, q % 2, :],
                                                 in0=cu8[:, i % 4, :],
                                                 scalar1=2.0, scalar2=-1.0,
                                                 op0=ALU.mult, op1=ALU.add)
                        tick(f'cvD:{fam}:{i}')

                kconv_range(0, 32)

                for c in range(NCHUNK):
                    for tc in range(NTC):
                        w_(f'ln1s:{c}:{tc}')
                        vector.tensor_scalar(out=mu_c[:, tc:tc + 1],
                                             in0=stats[:, 0:1],
                                             scalar1=1.0 / D, scalar2=None,
                                             op0=ALU.mult)
                        vector.tensor_scalar(out=stats[:, 2:3],
                                             in0=stats[:, 1:2],
                                             scalar1=1.0 / D, scalar2=None,
                                             op0=ALU.mult)
                        fence()
                        vector.tensor_scalar(out=stats[:, 1:2],
                                             in0=mu_c[:, tc:tc + 1],
                                             scalar1=mu_c[:, tc:tc + 1],
                                             scalar2=None, op0=ALU.mult)
                        fence()
                        vector.tensor_sub(out=stats[:, 2:3], in0=stats[:, 2:3],
                                          in1=stats[:, 1:2])
                        fence()
                        vector.tensor_scalar(out=stats[:, 2:3],
                                             in0=stats[:, 2:3],
                                             scalar1=float(EPS), scalar2=None,
                                             op0=ALU.add)
                        tick(f'ln1v:{c}:{tc}')
                        w_(f'ln1r:{c}:{tc}')
                        vector.tensor_scalar(
                            out=zb[:, tc % 2, :], in0=xst[:, tc % 2, :],
                            scalar1=mu_c[:, tc:tc + 1],
                            scalar2=rs_c[:, tc:tc + 1],
                            op0=ALU.subtract, op1=ALU.mult)
                        tick([f'xn32:{c}:{tc}', f'xBc:{c}:{tc}'])
                    for tc in range(NTC):
                        w_(f'tpg:{c}:{tc}',
                           f'opg:{c-1}:3:1' if (c > 0 and tc == 0) else None)
                        for h in range(8):
                            bk = (tc % 2) + (0 if h < 4 else 2)
                            vector.tensor_copy(
                                out=fybuf[:, h, tc * 128:(tc + 1) * 128],
                                in_=pm[bk][:, (h % 4) * 128:(h % 4 + 1) * 128])
                        tick(f'tpe:{c}:{tc}')
                    for g in range(NG):
                        if c % 4 == 0:
                            w_(f'convd:{c-1}:{g+6}' if (c > 0 and g < 2) else None)
                            vector.memset(xssm[:, g % 2, 0:3], 0.0)
                            tick(f'padc:{c}:{g}')
                        else:
                            w_(f'pads:{c-1}:{g}')
                            vector.tensor_copy(out=xssm[:, g % 2, 0:3],
                                               in_=pads[:, g, :])
                            tick(f'padc:{c}:{g}')
                        w_(f'ipe:{c}:{g}')
                        if (c + 1) % 4 != 0:
                            vector.tensor_copy(out=pads[:, g, :],
                                               in_=xssm[:, g % 2, 512:515])
                            tick(f'pads:{c}:{g}')
                        w_(f'xcbd:{c}:{g-2}' if g >= 2 else None,
                           f'padc:{c}:{g}')
                        vector.tensor_scalar(
                            out=convacc[:, g % 2, :],
                            in0=xssm[:, g % 2, 0:512],
                            scalar1=cst[:, OCW + 4 * g:OCW + 4 * g + 1],
                            scalar2=None, op0=ALU.mult)
                        for kk in range(1, 4):
                            vector.scalar_tensor_tensor(
                                out=convacc[:, g % 2, :],
                                in0=xssm[:, g % 2, kk:kk + 512],
                                scalar=cst[:, OCW + 4 * g + kk:OCW + 4 * g + kk + 1],
                                in1=convacc[:, g % 2, :],
                                op0=ALU.mult, op1=ALU.add)
                        tick([f'conv4:{c}:{g}', f'convd:{c}:{g}'])
                        w_(f'xcbd:{c}:{g}',
                           f'yadd:{c-1}:127' if (c > 0 and g == 0) else None)
                        vector.tensor_scalar(out=yacc[:, g, :],
                                             in0=xcb[:, g, :],
                                             scalar1=cst[:, OD + g:OD + g + 1],
                                             scalar2=None, op0=ALU.mult)
                        tick(f'yini:{c}:{g}')
                    for g in range(NG):
                        w_(f'lbfd:{c}:{g}')
                        vector.tensor_mul(out=dtxb[:, g, :], in0=xcb[:, g, :],
                                          in1=lbf[:, g, :])
                        tick(f'dtxd:{c}:{g}')
                    if c == 0:
                        kconv_range(32, len(KLIST))
                    def gbuild(n):
                        for kind, row in ((0, n), (1, 16 + n)):
                            w_(f'bce:{c}' if (n == 0 and kind == 0) else None,
                               f'bcmm{"B" if kind == 0 else "C"}:{c}:{n-1}'
                               if n >= 1 else None)
                            vector.tensor_scalar(
                                out=mrhs[:, kind, :],
                                in0=bcst[:, c % 2, :],
                                scalar1=ident[0:32, row:row + 1],
                                scalar2=None, op0=ALU.mult)
                            evs = [f'mrhs{"B" if kind == 0 else "C"}:{c}:{n}']
                            if n == NN - 1 and kind == 1:
                                evs.append(f'mrhsd:{c}:{2*NN-1}')
                            b.inc(vector.last, 'dve', 1, ev=evs)

                    for n0 in range(4):
                        gbuild(n0)
                    for i in range(128):
                        n, g = divmod(i, NG)
                        if i >= 10 and (i - 10) % 8 == 0 and (i - 10) // 8 + 4 < NN:
                            gbuild((i - 10) // 8 + 4)
                        w_(f'bcL:{c}:{n}' if g == 0 else None,
                           f'dtxd:{c}:{g}' if n == 0 else None)
                        vector.tensor_mul(
                            out=ub[:, :],
                            in0=dtxb[:, g, :], in1=negb[:, n % 3, :])
                        if g == NG - 1:
                            tick(f'bcuseB:{c}:{n}')
                        w_(f'dA:{c}:{i}',
                           f'ymul:{c}:{i-2}' if i >= 2 else
                           (f'ymul:{c-1}:{126+i}' if c > 0 else None),
                           f'carrydone:{c-1}' if (i == 0 and c % 4 != 0) else None,
                           f'hcclear:{c}' if (i == 0 and c % 4 == 0) else None)
                        ins = vector.tensor_tensor_scan(
                            out=hb[:, i % 2, :], data0=da[:, i % 2, :],
                            data1=ub[:, :],
                            initial=hc[:, g * 16 + n:g * 16 + n + 1],
                            op0=ALU.mult, op1=ALU.add)
                        b.inc(ins, 'dve', 1, ev=f'scank:{c}:{i}')
                        if i >= 1:
                            w_(f'ymul:{c}:{i-1}')
                            gp = (i - 1) % NG
                            vector.tensor_add(out=yacc[:, gp, :],
                                              in0=yacc[:, gp, :],
                                              in1=pb[:, (i - 1) % 2, :])
                            tick(f'yadd:{c}:{i-1}')
                    w_(f'ymul:{c}:127')
                    vector.tensor_add(out=yacc[:, 7, :], in0=yacc[:, 7, :],
                                      in1=pb[:, 1, :])
                    tick(f'yadd:{c}:127')
                    for g in range(NG):
                        w_(f'ffn0done:{c}' if g == 0 else None)
                        vector.tensor_mul(out=fybuf[:, g, :],
                                          in0=yacc[:, g, :],
                                          in1=gateb[:, g, :])
                        tick([f'ybf:{c}:{g}'] +
                             ([f'ybfall:{c}'] if g == NG - 1 else []))
                    for tc in range(NTC):
                        w_(f'xE:{c}:{tc}',
                           f'tpg:{c}:{tc+2}' if tc < 2 else None,
                           f'mst:{c}:{tc-2}' if tc >= 2 else None)
                        vector.tensor_scalar(
                            out=zb[:, tc % 2, :], in0=xst[:, tc % 2, :],
                            scalar1=mu_c[:, tc:tc + 1],
                            scalar2=rs_c[:, tc:tc + 1],
                            op0=ALU.subtract, op1=ALU.mult)
                        fence()
                        if c == 0 and tc == 0:
                            w_('ld_reps')
                        for h in range(2):
                            w_(f'opg:{c}:{tc}:{h}')
                            vector.tensor_add(
                                out=zb[:, tc % 2, h * 512:(h + 1) * 512],
                                in0=zb[:, tc % 2, h * 512:(h + 1) * 512],
                                in1=pm[tc * 2 + h][:, :])
                            fence()
                        vector.tensor_add(out=zb[:, tc % 2, :],
                                          in0=zb[:, tc % 2, :],
                                          in1=outbr[:, :])
                        tick([f'zrdy:{c}:{tc}', f'ope:{c}:{tc}'])
                        w_(f'ln2s:{c}:{tc}')
                        vector.tensor_add(out=stats[:, 3:4], in0=stats[:, 3:4],
                                          in1=stats[:, 4:5])
                        vector.tensor_add(out=stats[:, 5:6], in0=stats[:, 5:6],
                                          in1=stats[:, 6:7])
                        fence()
                        vector.tensor_scalar(out=stats[:, 3:4],
                                             in0=stats[:, 3:4],
                                             scalar1=1.0 / D, scalar2=None,
                                             op0=ALU.mult)
                        vector.tensor_scalar(out=stats[:, 5:6],
                                             in0=stats[:, 5:6],
                                             scalar1=1.0 / D, scalar2=None,
                                             op0=ALU.mult)
                        fence()
                        vector.tensor_scalar(out=stats[:, 4:5],
                                             in0=stats[:, 3:4],
                                             scalar1=stats[:, 3:4],
                                             scalar2=None, op0=ALU.mult)
                        fence()
                        vector.tensor_sub(out=stats[:, 7:8], in0=stats[:, 5:6],
                                          in1=stats[:, 4:5])
                        fence()
                        vector.tensor_scalar(out=stats[:, 7:8],
                                             in0=stats[:, 7:8],
                                             scalar1=float(EPS), scalar2=None,
                                             op0=ALU.add)
                        tick(f'ln2v:{c}:{tc}')
                        w_(f'ln2r:{c}:{tc}')
                        vector.tensor_scalar(
                            out=zb[:, tc % 2, :], in0=zb[:, tc % 2, :],
                            scalar1=stats[:, 3:4], scalar2=stats[:, 9:10],
                            op0=ALU.subtract, op1=ALU.mult)
                        fence()
                        vector.tensor_mul(out=zb[:, tc % 2, :],
                                          in0=zb[:, tc % 2, :],
                                          in1=lngr[:, :])
                        fence()
                        vector.tensor_add(out=zb[:, tc % 2, :],
                                          in0=zb[:, tc % 2, :],
                                          in1=lnbr[:, :])
                        fence()
                        vector.tensor_add(out=zb[:, tc % 2, :],
                                          in0=zb[:, tc % 2, :],
                                          in1=xst[:, tc % 2, :])
                        tick([f'zdone:{c}:{tc}', f'xEc:{c}:{tc}'])
                if DEBUG_DUMP and c == NCHUNK - 1:
                    for bk in range(8):
                        if bk > 0:
                            b.wt(vector, f'dpmst:{bk-1}')
                        vector.tensor_copy(out=sdt[:, :], in_=pm[bk][:, :])
                        tick(f'dpm:{bk}')

            @block.gpsimd
            def _(gpsimd):
                gpsimd = EngWrap(gpsimd)

                def w_(*evs):
                    b.wts(gpsimd, [e for e in evs if e])

                def tick(ev):
                    b.inc(gpsimd.last, 'gps', 1, ev=ev)

                gpsimd.memset(ident[:, :], 0.0)
                b.inc(gpsimd.last, 'gps', 1)
                gpsimd.wait_ge(b.sems['gps'], b.counts['gps'])
                gpsimd.affine_select(out=ident[:, :], in_=ident[:, :],
                                     compare_op=ALU.not_equal, fill=1.0,
                                     base=0, pattern=[[-1, 128]],
                                     channel_multiplier=1)
                tick('identR')
                gpsimd.memset(hc[:, :], 0.0)
                tick('hcclear:0')
                gpsimd.memset(ones32[:, :], 1.0)
                tick('ones32R')
                b.inc(gpsimd.dma_start(out=outbr[:, :], in_=outb_h), 'qr', 16)
                b.inc(gpsimd.dma_start(out=lngr[:, :], in_=lng_h), 'qr', 16)
                b.inc(gpsimd.dma_start(out=lnbr[:, :], in_=lnb_h), 'qr', 16)
                b.mark('qr', 'ld_reps')

                qg_pieces = []
                for j in range(16):
                    qg_pieces.append((win_h[:, j, :, :]
                                      .rearrange("p k m -> p (k m)"),
                                      win_s[j, :, :], f'cvS:win:{j}'))
                for j in range(8):
                    qg_pieces.append((wdt_h[:, j, :, :]
                                      .rearrange("p k m -> p (k m)"),
                                      wdt_s[j, :, :], f'cvS:wdt:{j}'))
                for j in range(8):
                    h_, k_ = divmod(2 * j, 8)
                    qg_pieces.append((wout_h
                                      .rearrange("p h k o -> p (h k o)")
                                      [:, j * 1024:(j + 1) * 1024],
                                      wout_s[h_, :, k_ * 512:(k_ + 2) * 512],
                                      f'cvS:wout:{j}'))
                NQG = len(qg_pieces)

                for g in (0, 1):
                    b.inc(gpsimd.dma_start(out=cbf_slot(g % 4),
                                           in_=qg_pieces[g][0]),
                          f'qgl{g % 4}', 16, ev=f'cvLg:{g}')
                for g in range(NQG):
                    if g + 2 < NQG:
                        w_(qg_pieces[g - 2][2] if g >= 2 else None)
                        b.inc(gpsimd.dma_start(out=cbf_slot((g + 2) % 4),
                                               in_=qg_pieces[g + 2][0]),
                              f'qgl{(g + 2) % 4}', 16, ev=f'cvLg:{g+2}')
                    w_(f'cvLg:{g}')
                    b.inc(gpsimd.dma_start(out=qg_pieces[g][1],
                                           in_=cbf_slot(g % 4)),
                          f'qgs{g % 4}', 16, ev=qg_pieces[g][2])

                for c in range(NCHUNK):
                    s, j0 = c // 4, (c % 4)
                    tok0 = j0 * TC
                    if c == 4:
                        w_(f'scank:3:127')
                        gpsimd.memset(hc[:, :], 0.0)
                        tick('hcclear:4')
                    for i in range(128):
                        n, g = divmod(i, NG)
                        w_(f'scank:{c}:{i}',
                           f'yadd:{c}:{i-2}' if i >= 2 else
                           (f'yadd:{c-1}:{126+i}' if c > 0 else None))
                        gpsimd.tensor_mul(out=pb[:, i % 2, :],
                                          in0=hb[:, i % 2, :],
                                          in1=cbr[:, n % 3, :])
                        ins = gpsimd.tensor_copy(
                            out=hc[:, g * 16 + n:g * 16 + n + 1],
                            in_=hb[:, i % 2, 511:512])
                        evs = [f'ymul:{c}:{i}']
                        if g == NG - 1:
                            evs.append(f'bcuseC:{c}:{n}')
                        if i == 127:
                            evs.append(f'carrydone:{c}')
                        b.inc(ins, 'gps', 1, ev=evs)
                    for tc in range(NTC):
                        w_(f'zdone:{c}:{tc}', f'f2e:{c}:{1}:{tc}')
                        gpsimd.tensor_tensor(
                            out=zb[:, tc % 2, :], in0=zb[:, tc % 2, :],
                            in1=baseb[:, tc * 2:tc * 2 + 2, :]
                            .rearrange("p a t -> p (a t)"),
                            op=ALU.add)
                        b.inc(gpsimd.last, 'gps', 1, ev=f'mcp:{c}:{tc}')
                        t0 = tok0 + tc * 128
                        b.inc(gpsimd.dma_start(
                            out=out_hbm[s, t0:t0 + 128, :],
                            in_=zb[:, tc % 2, :]), f'qst{tc % 2}', 16,
                            ev=f'mst:{c}:{tc}')

                if DEBUG_DUMP:
                    for bk in range(8):
                        b.wt(gpsimd, f'dpm:{bk}')
                        b.inc(gpsimd.dma_start(
                            out=dbg["d_pm"][:, bk, :], in_=sdt[:, :]), 'qst0', 16,
                            ev=f'dpmst:{bk}')
                    for nm, sbuf in [("d_xcb", xcb), ("d_lbf", lbf),
                                     ("d_gate", gateb), ("d_fy", fybuf),
                                     ("d_yacc", yacc), ("d_h1", h1),
                                     ("d_h2", h2), ("d_dtx", dtxb),
                                     ("d_zb", zb), ("d_negb", negb),
                                     ("d_outbr", outbr), ("d_lngr", lngr),
                                     ("d_lnbr", lnbr), ("d_base", baseb),
                                     ("d_rout", rout), ("d_hc", hc),
                                     ("d_xst", xst), ("d_rw1", rw1),
                                     ("d_rw0", rw0),
                                     ("d_cbr", cbr), ("d_stats", stats),
                                     ("d_mu", mu_c), ("d_rs", rs_c)]:
                        b.inc(gpsimd.dma_start(out=dbg[nm],
                                               in_=sbuf[...]), 'qst0', 16)
                for s_ in ('qst0', 'qst1'):
                    gpsimd.wait_ge(b.sems[s_], b.counts[s_])

    return nc, b


_CACHE = {}


def _get_program():
    if 'nc' not in _CACHE:
        _, bA = build_program(None)
        nc, _ = build_program(bA.events)
        _CACHE['nc'] = nc
    return _CACHE['nc']


def make_inputs(inputs):
    x = np.ascontiguousarray(np.asarray(inputs['L0_attn_x'], dtype=np.float32))
    ipw = np.asarray(inputs['L0_attn_in_proj_w'], dtype=np.float32)
    ipb = np.asarray(inputs['L0_attn_in_proj_b'], dtype=np.float32)
    cw = np.asarray(inputs['L0_attn_conv_w'], dtype=np.float32)[:, 0, :]
    cb = np.asarray(inputs['L0_attn_conv_b'], dtype=np.float32)
    bcw = np.asarray(inputs['L0_attn_bc_w'], dtype=np.float32)
    bcb = np.asarray(inputs['L0_attn_bc_b'], dtype=np.float32)
    dtw = np.asarray(inputs['L0_attn_dt_w'], dtype=np.float32)
    dtb = np.asarray(inputs['L0_attn_dt_b'], dtype=np.float32)
    alog = np.asarray(inputs['L0_attn_A_log'], dtype=np.float32)
    Dv = np.asarray(inputs['L0_attn_D'], dtype=np.float32)
    ow = np.asarray(inputs['L0_attn_out_w'], dtype=np.float32)
    ob = np.asarray(inputs['L0_attn_out_b'], dtype=np.float32)
    lng = np.asarray(inputs['L0_attn_ln_g'], dtype=np.float32)
    lnb = np.asarray(inputs['L0_attn_ln_b'], dtype=np.float32)
    k0 = np.asarray(inputs['L0_ffn_k0'])
    k1 = np.asarray(inputs['L0_ffn_k1'])
    k2 = np.asarray(inputs['L0_ffn_k2'])
    s0 = np.asarray(inputs['L0_ffn_s0'], dtype=np.float32)
    s1 = np.asarray(inputs['L0_ffn_s1'], dtype=np.float32)
    s2 = np.asarray(inputs['L0_ffn_s2'], dtype=np.float32)

    win_h = np.ascontiguousarray(
        ipw.reshape(16, 128, 8, 128).transpose(3, 0, 2, 1))
    wdt_h = np.ascontiguousarray(
        dtw.reshape(8, 128, 8, 128).transpose(3, 0, 2, 1))
    wout_h = np.ascontiguousarray(
        ow.reshape(2, 512, 8, 128).transpose(3, 0, 2, 1))
    wbc_h = np.ascontiguousarray(bcw.reshape(32, 8, 128).transpose(2, 1, 0))
    k0_h = np.ascontiguousarray(k0.view(np.uint8).reshape(2, 8, 128, 4096))
    k1_h = np.ascontiguousarray(k1.view(np.uint8).reshape(32, 128, 4096))
    k2_h = np.ascontiguousarray(k2.view(np.uint8).reshape(32, 128, 1024))
    cst_h = np.zeros((128, 204), np.float32)
    cst_h[:, 0:16] = ipb.reshape(16, 128).T
    cst_h[:, 16:48] = cw.reshape(8, 128, 4).transpose(1, 0, 2).reshape(128, 32)
    cst_h[:, 48:56] = cb.reshape(8, 128).T
    cst_h[:, 56:64] = dtb.reshape(8, 128).T
    cst_h[0:32, 64] = bcb
    cst_h[:, 65:73] = Dv.reshape(8, 128).T
    cst_h[:, 73:76] = np.stack([s0, s1, s2])[None, :]
    cst_h[:, 76:204] = alog.reshape(8, 128, 16).transpose(1, 0, 2).reshape(128, 128)
    outb_h = np.ascontiguousarray(np.broadcast_to(ob, (128, 1024)))
    lng_h = np.ascontiguousarray(np.broadcast_to(lng, (128, 1024)))
    lnb_h = np.ascontiguousarray(np.broadcast_to(lnb, (128, 1024)))

    shared = dict(win_h=win_h, wdt_h=wdt_h, wout_h=wout_h, wbc_h=wbc_h,
                  k0_h=k0_h, k1_h=k1_h, k2_h=k2_h, cst_h=cst_h,
                  outb_h=outb_h, lng_h=lng_h, lnb_h=lnb_h)
    in_maps = []
    for i in range(NCORE):
        m = dict(shared)
        m['x_in'] = np.ascontiguousarray(x[2 * i:2 * i + 2])
        in_maps.append(m)

    return in_maps


def kernel(**inputs):
    in_maps = make_inputs(inputs)
    nc = _get_program()
    res = run_bass_kernel_spmd(nc, in_maps, core_ids=list(range(NCORE)))
    out = np.concatenate([r['out'] for r in res.results], axis=0)
    return out.reshape(16, 2048, 1024).astype(np.float32)


# version 19
# speedup vs baseline: 1.0605x; 1.0605x over previous
import numpy as np
from contextlib import ExitStack

import concourse.bass as bass
import concourse.mybir as mybir
from concourse.bass_utils import run_bass_kernel_spmd

F32 = mybir.dt.float32
BF16 = mybir.dt.bfloat16
U8 = mybir.dt.uint8
AF = mybir.ActivationFunctionType
ALU = mybir.AluOpType

NCORE = 8
BLOC = 2
L = 2048
D = 1024
NCHUNK = 8
TC = 512
NTC = 4
NG = 8
NN = 16
EPS = 1e-5
W1_RING = 32
W1_BATCH = 16
DEBUG_DUMP = False
NO_BCAST = False


class Bld:
    def __init__(self, known):
        self.known = known
        self.events = {}
        names = ['pe', 'act', 'dve', 'gps', 'qs', 'qg', 'qr',
                 'qcv0a', 'qcv0b', 'qcv0c', 'qcv0d',
                 'qcv1a', 'qcv1b', 'qcv1c', 'qcv1d',
                 'qcv2a', 'qcv2b', 'qcv2c', 'qcv2d',
                 'qgl0', 'qgl1', 'qgl2', 'qgl3',
                 'qgs0', 'qgs1', 'qgs2', 'qgs3',
                 'qx0', 'qx1', 'qwin0', 'qwin1', 'qdt0', 'qdt1',
                 'qw00', 'qw01', 'qw10', 'qw11', 'qw12', 'qw13',
                 'qw20', 'qw21', 'qwo0', 'qwo1',
                 'qcu0', 'qcu1', 'qcu2', 'qcu3',
                 'qst0', 'qst1']
        self.counts = {k: 0 for k in names}
        self.sem_names = names
        self.sems = {}

    def inc(self, instr, sem, n=1, ev=None):
        self.counts[sem] += n
        if ev is not None:
            if isinstance(ev, str):
                ev = [ev]
            for e in ev:
                assert e not in self.events, f"duplicate event {e}"
                self.events[e] = (sem, self.counts[sem])
        instr.then_inc(self.sems[sem], n)

    def mark(self, sem, ev):
        if isinstance(ev, str):
            ev = [ev]
        for e in ev:
            assert e not in self.events, f"duplicate event {e}"
            self.events[e] = (sem, self.counts[sem])

    def wt(self, eng, ev):
        if not ev:
            return
        if self.known is not None:
            sem, val = self.known[ev]
            eng.wait_ge(self.sems[sem], val)
        elif ev in self.events:
            sem, val = self.events[ev]
            eng.wait_ge(self.sems[sem], val)

    def wts(self, eng, evs):
        for e in evs:
            self.wt(eng, e)


def ffn_schedule():
    items = []
    for o in range(32):
        items.append((4 + 2 * o, 'f0', o))
    for o in range(32):
        items.append((70 + (o * 18) // 10, 'f1', o))
    for n in range(4, 16):
        items.append((8 * n - 6, 'bc', n))
    items.sort(key=lambda t: (t[0], t[1]))
    return items


KLIST = ([('k0', i) for i in range(32)] +
         [('k1', i) for i in range(128)] +
         [('k2', i) for i in range(32)])


def kmpos(q):
    return 7 * (q // 6) + 1 + q % 6


def qgpos(g):
    return 7 * g


class EngWrap:

    def __init__(self, eng):
        self._eng = eng
        self.last = None

    def __getattr__(self, name):
        f = getattr(self._eng, name)
        if not callable(f):
            return f
        def wrap(*a, **kw):
            r = f(*a, **kw)
            if name not in ('wait_ge', 'then_inc'):
                self.last = r
            return r
        return wrap


def build_program(known):
    nc = bass.Bass("TRN2", target_bir_lowering=False,
                   detect_race_conditions=False)
    b = Bld(known)

    dt_ = nc.dram_tensor
    x_in = dt_("x_in", [BLOC, L, D], F32, kind="ExternalInput").ap()
    win_h = dt_("win_h", [128, 16, 8, 128], F32, kind="ExternalInput").ap()
    wdt_h = dt_("wdt_h", [128, 8, 8, 128], F32, kind="ExternalInput").ap()
    wout_h = dt_("wout_h", [128, 2, 8, 512], F32, kind="ExternalInput").ap()
    wbc_h = dt_("wbc_h", [128, 8, 32], F32, kind="ExternalInput").ap()
    k0_h = dt_("k0_h", [2, 8, 128, 4096], U8, kind="ExternalInput").ap()
    k1_h = dt_("k1_h", [32, 128, 4096], U8, kind="ExternalInput").ap()
    k2_h = dt_("k2_h", [32, 128, 1024], U8, kind="ExternalInput").ap()
    cst_h = dt_("cst_h", [128, 204], F32, kind="ExternalInput").ap()
    outb_h = dt_("outb_h", [128, 1024], F32, kind="ExternalInput").ap()
    lng_h = dt_("lng_h", [128, 1024], F32, kind="ExternalInput").ap()
    lnb_h = dt_("lnb_h", [128, 1024], F32, kind="ExternalInput").ap()
    out_hbm = dt_("out", [BLOC, L, D], F32, kind="ExternalOutput").ap()

    dbg = {}
    if DEBUG_DUMP:
        for nm, shp, dty in [("d_xcb", [128, 8, 512], BF16),
                             ("d_lbf", [128, 8, 512], BF16),
                             ("d_gate", [128, 8, 512], BF16),
                             ("d_fy", [128, 8, 512], BF16),
                             ("d_yacc", [128, 8, 512], BF16),
                             ("d_h1", [128, 32, 512], BF16),
                             ("d_h2", [128, 32, 512], BF16),
                             ("d_dtx", [128, 8, 512], BF16),
                             ("d_zb", [128, 2, 1024], F32),
                             ("d_negb", [128, 4, 512], BF16),
                             ("d_cbr", [128, 4, 512], BF16),
                             ("d_stats", [128, 10], F32),
                             ("d_outbr", [128, 1024], BF16),
                             ("d_lngr", [128, 1024], BF16),
                             ("d_lnbr", [128, 1024], BF16),
                             ("d_base", [128, 8, 512], BF16),
                             ("d_rout", [128, 4, 512], BF16),
                             ("d_hc", [128, 128], F32),
                             ("d_xst", [128, 2, 1024], F32),
                             ("d_pm", [128, 8, 512], F32),
                             ("d_zall", [128, 4, 1024], F32),
                             ("d_ball", [128, 8, 512], F32),
                             ("zlog", [NCHUNK, 4, 128, 1024], F32),
                             ("blog", [NCHUNK, 8, 128, 512], F32),
                             ("d_rw1", [128, 32, 128], BF16),
                             ("d_rw0", [128, 2, 8, 128], BF16),
                             ("d_mu", [128, 4], F32),
                             ("d_rs", [128, 4], F32)]:
            dbg[nm] = dt_(nm, shp, dty, kind="ExternalOutput").ap()
    win_s = dt_("win_s", [16, 128, 1024], BF16).ap()
    wdt_s = dt_("wdt_s", [8, 128, 1024], BF16).ap()
    wout_s = dt_("wout_s", [2, 128, 4096], BF16).ap()
    w0_s = dt_("w0_s", [32, 128, 1024], BF16).ap()
    w1_s = dt_("w1_s", [32, 128, 4096], BF16).ap()
    w2_s = dt_("w2_s", [2, 128, 16384], BF16).ap()

    with ExitStack() as st:
        sb = lambda name, shape, dt: st.enter_context(
            nc.sbuf_tensor(name, shape, dt))
        ps = lambda name, shape, dt: st.enter_context(
            nc.psum_tensor(name, shape, dt))
        for s_ in b.sem_names:
            b.sems[s_] = st.enter_context(nc.semaphore(s_))

        ident = sb("ident", [128, 128], F32)
        apos = sb("apos", [128, 128], F32)
        cst = sb("cst", [128, 204], F32)
        bcsc = sb("bcsc", [32, 1], F32)
        outbr = sb("outbr", [128, 1024], BF16)
        lngr = sb("lngr", [128, 1024], BF16)
        lnbr = sb("lnbr", [128, 1024], BF16)
        scol = sb("scol", [128, 3], F32)
        stats = sb("stats", [128, 10], F32)
        mu_c = sb("mu_c", [128, 4], F32)
        rs_c = sb("rs_c", [128, 4], F32)
        rin = sb("rin", [128, 2, 8, 128], BF16)
        rdt = sb("rdt", [128, 2, 8, 128], BF16)
        rout = sb("rout", [128, 4, 512], BF16)
        rw0 = sb("rw0", [128, 2, 8, 128], BF16)
        rw1 = sb("rw1", [128, W1_RING, 128], BF16)
        rw2 = sb("rw2", [128, 4, 512], BF16)
        rbc = sb("rbc", [128, 8, 32], BF16)
        xcb = sb("xcb", [128, 8, 512], BF16)
        gateb = sb("gateb", [128, 8, 512], BF16)
        lbf = sb("lbf", [128, 8, 512], BF16)
        dtxb = sb("dtxb", [128, 8, 512], BF16)
        yacc = sb("yacc", [128, 8, 512], BF16)
        fybuf = sb("fybuf", [128, 8, 512], BF16)
        hc = sb("hc", [128, 128], F32)
        sdt = sb("sdt", [128, 512], F32)
        convacc = sb("convacc", [128, 2, 512], F32)
        xssm = sb("xssm", [128, 2, 515], F32)
        pads = sb("pads", [128, 8, 3], F32)
        negb = sb("negb", [128, 3, 512], BF16)
        cbr = sb("cbr", [128, 3, 512], BF16)
        da = sb("da", [128, 2, 512], F32)
        ub = sb("ub", [128, 512], BF16)
        hb = sb("hb", [128, 2, 512], F32)
        pb = sb("pb", [128, 2, 512], F32)
        h1 = sb("h1", [128, 32, 512], BF16)
        h2 = sb("h2", [128, 32, 512], BF16)
        xst = sb("xst", [128, 2, 1024], F32)
        zb = sb("zb", [128, 2, 1024], F32)
        baseb = sb("baseb", [128, 8, 512], BF16)
        bcst = sb("bcst", [32, 2, 512], BF16)
        mrhs = sb("mrhs", [32, 2, 512], BF16)
        ones32 = sb("ones32", [32, 128], BF16)
        cu8 = baseb[:, 0:4, :].rearrange("p a t -> p (a t)") \
            .bitcast(U8).rearrange("p (a t) -> p a t", a=4)
        cbfA = rw2[:, :, :].rearrange("p a t -> p (a t)") \
            .rearrange("p (a b) -> p a b", a=2)
        cbfB = rout[:, :, :].rearrange("p a t -> p (a t)") \
            .rearrange("p (a b) -> p a b", a=2)
        cbf_slot = lambda s: (cbfA if s < 2 else cbfB)[:, s % 2, :]
        cb2 = baseb[:, 4:8, :].rearrange("p a t -> p (a t)") \
            .rearrange("p (a b) -> p a b", a=2)
        cb2B = da[:, :, :].rearrange("p a t -> p (a t)") \
            .bitcast(BF16).rearrange("p (a b) -> p a b", a=2)
        cb2_slot = lambda s: (cb2 if s < 2 else cb2B)[:, s % 2, :]

        OB, OCW, OCB, ONDT, OBC, OD, OS, OA = 0, 16, 48, 56, 64, 65, 73, 76
        pm = [ps(f"pm{i}", [128, 512], F32) for i in range(8)]
        bank_w = [None] * 8
        bank_r = [None] * 8

        with nc.Block() as block:

            @block.sync
            def _(sync):
                def dma(dst, src, ev=None, wait=(), q='qs'):
                    if isinstance(wait, str):
                        wait = (wait,)
                    b.wts(sync, [w for w in wait if w])
                    b.inc(sync.dma_start(out=dst, in_=src), q, 16, ev=ev)

                dma(cst[:, :], cst_h, ev='ld_small')
                dma(sdt[:, 0:256], wbc_h.rearrange("p g m -> p (g m)"),
                    ev='ld_wbc', wait='ld_small')

                dma(xst[:, 0, :], x_in[0, 0:128, :], ev='xB:0:0', q='qx0')
                dma(xst[:, 1, :], x_in[0, 128:256, :], ev='xB:0:1', q='qx1')

                def w1_load(c, bi):
                    t0 = bi * W1_BATCH
                    o_chunk, kk = divmod(t0, 32)
                    w = []
                    if bi >= W1_RING // W1_BATCH:
                        w.append(f'w1b:{c}:{bi - W1_RING // W1_BATCH}')
                    elif c > 0:
                        w.append(f'w1b:{c-1}:{bi + 64 - W1_RING // W1_BATCH}')
                    if c == 0:
                        w += [f'cvS:k1:{(o_chunk // 8) * 32 + t}'
                              for t in (28, 29, 30, 31)]
                    sl = t0 % W1_RING
                    dma(rw1[:, sl:sl + W1_BATCH, :],
                        w1_s[o_chunk, :,
                             kk * 128:(kk + W1_BATCH) * 128]
                        .rearrange("p (k m) -> p k m", k=W1_BATCH),
                        ev=f'w1L:{c}:{bi}', wait=w, q=f'qw1{bi % 4}')

                def kload(q):
                    fam, i = KLIST[q]
                    if fam == 'k0':
                        j, k = divmod(i, 8)
                        for half in (0, 1):
                            dma(cu8[:, (2 * i + half) % 4, :],
                                k0_h[half, k, :, j * 1024:(j + 1) * 1024],
                                ev=f'cvL{half}:k0:{i}',
                                wait=(f'cvD:k0:{i-2}',) if i >= 2 else (),
                                q=f'qcu{(2 * i + half) % 4}')
                    elif fam == 'k1':
                        j, k = divmod(i, 32)
                        dma(cu8[:, i % 4, :],
                            k1_h[k, :, j * 1024:(j + 1) * 1024],
                            ev=f'cvL:k1:{i}',
                            wait=(f'cvD:k1:{i-4}',) if i >= 4
                            else (f'cvD:k0:{30 + i // 2}',),
                            q=f'qcu{i % 4}')
                    else:
                        dma(cu8[:, i % 4, :], k2_h[i, :, :],
                            ev=f'cvL:k2:{i}',
                            wait=(f'cvD:k2:{i-4}',) if i >= 4
                            else (f'cvD:k1:{124 + i}',),
                            q=f'qcu{i % 4}')

                def kstore(q):
                    fam, i = KLIST[q]
                    b.wt(sync, f'cvD:{fam}:{i}')
                    if fam == 'k0':
                        j, k = divmod(i, 8)
                        out = w0_s[j * 8:(j + 1) * 8, :,
                                   k * 128:(k + 1) * 128] \
                            .rearrange("o p m -> p o m")
                        in_ = cb2_slot(q % 4).rearrange("p (o m) -> p o m",
                                                        o=8)
                        sem = 'qcv0' + 'abcd'[i % 4]
                    elif fam == 'k1':
                        j, k = divmod(i, 32)
                        out = w1_s[j * 8:(j + 1) * 8, :,
                                   k * 128:(k + 1) * 128] \
                            .rearrange("o p m -> p o m")
                        in_ = cb2_slot(q % 4).rearrange("p (o m) -> p o m",
                                                        o=8)
                        sem = 'qcv1' + 'abcd'[i % 4]
                    else:
                        out = w2_s[:, :, i * 512:(i + 1) * 512] \
                            .rearrange("h p o -> p h o")
                        in_ = cb2_slot(q % 4).rearrange("p (h o) -> p h o",
                                                        h=2)
                        sem = 'qcv2' + 'abcd'[i % 4]
                    b.inc(sync.dma_start(out=out, in_=in_), sem, 16,
                          ev=f'cvS:{fam}:{i}')

                def chunk_zipper(c):
                    s, j0 = c // 4, (c % 4)
                    tok0 = j0 * TC
                    items = []
                    if c > 0:
                        for tc in (0, 1):
                            items.append((-300 + tc, lambda tc=tc: dma(
                                xst[:, tc, :],
                                x_in[s, tok0 + tc * 128:tok0 + (tc + 1) * 128, :],
                                ev=f'xB:{c}:{tc}',
                                wait=(f'xEc:{c-1}:{tc+2}',), q=f'qx{tc}')))
                        for tc in (2, 3):
                            items.append((-298 + tc, lambda tc=tc: dma(
                                xst[:, tc % 2, :],
                                x_in[s, tok0 + tc * 128:tok0 + (tc + 1) * 128, :],
                                ev=f'xB:{c}:{tc}',
                                wait=(f'xBc:{c}:{tc-2}',), q=f'qx{tc % 2}')))
                    else:
                        for tc in (2, 3):
                            items.append((-227.5 + 0.01 * tc, lambda tc=tc: dma(
                                xst[:, tc % 2, :],
                                x_in[0, tc * 128:(tc + 1) * 128, :],
                                ev=f'xB:0:{tc}', wait=(f'xBc:0:{tc-2}',),
                                q=f'qx{tc % 2}')))
                    post = lambda k: -90.0 + (k + 300.0) * 0.01
                    if c == 0:
                        for q in range(len(KLIST)):
                            if q < 32:
                                lk = -250 + 0.7 * max(q - 3, 0)
                                sk = -250 + 0.7 * q + 0.35
                            else:
                                lk = -227.0 + 0.55 * (max(q - 3, 32) - 32)
                                sk = -227.0 + 0.55 * (q - 32) + 0.3
                            items.append((lk, lambda q=q: kload(q)))
                            items.append((sk, lambda q=q: kstore(q)))
                    for o in range(16):
                        items.append(((-249 + 0.01 * o if o < 2
                                       else -227.4 + 0.005 * o) if c == 0
                                      else -200 + o,
                                      lambda o=o: ring_win1(c, o)))
                    for o in range(8):
                        items.append((-227.3 + 0.002 * o if c == 0
                                      else -100 + o,
                                      lambda o=o: ring_dt1(c, o)))
                    for o in range(32):
                        kk0 = post(-10 + 2 * o if o < 2 else 2 * o)
                        items.append((kk0 if c == 0
                                      else (-10 + 2 * o if o < 2 else 2 * o),
                                      lambda o=o: ring_w01(c, o)))
                    for bi in range(64):
                        kk = 30 if bi < 2 else 64 + (bi - 2) * 1.8
                        items.append((post(kk) if c == 0 else kk,
                                      lambda bi=bi: w1_load(c, bi)))
                    for bi in range(32):
                        items.append((post(300 + bi) if c == 0 else 300 + bi,
                                      lambda bi=bi: ring_w21(c, bi)))
                    for bi in range(8):
                        items.append((post(400 + bi) if c == 0 else 400 + bi,
                                      lambda bi=bi: ring_wout1(c, bi)))
                    for tc in range(NTC):
                        items.append((post(500 + tc) if c == 0 else 500 + tc,
                                      lambda tc=tc: dma(
                            xst[:, tc % 2, :],
                            x_in[s, tok0 + tc * 128:tok0 + (tc + 1) * 128, :],
                            ev=f'xE:{c}:{tc}',
                            wait=[f'xBc:{c}:{2 + (tc % 2)}'] +
                            ([f'xEc:{c}:{tc-2}'] if tc >= 2 else []),
                            q=f'qx{tc % 2}')))
                    items.sort(key=lambda t: t[0])
                    for _, f in items:
                        f()

                def ring_win1(c, o):
                    w = []
                    if o >= 2:
                        w.append(f'winoc:{c}:{o-2}')
                    elif c > 0:
                        w.append(f'winoc:{c-1}:{o+14}')
                    if c == 0:
                        w.append(f'cvS:win:{o}')
                    dma(rin[:, o % 2, :, :],
                        win_s[o, :, :].rearrange("p (k m) -> p k m", k=8),
                        ev=f'winL:{c}:{o}', wait=w, q=f'qwin{o % 2}')

                def ring_dt1(c, o):
                    w = []
                    if o >= 2:
                        w.append(f'dtoc:{c}:{o-2}')
                    elif c > 0:
                        w.append(f'dtoc:{c-1}:{o+6}')
                    if c == 0:
                        w.append(f'cvS:wdt:{o}')
                    dma(rdt[:, o % 2, :, :],
                        wdt_s[o, :, :].rearrange("p (k m) -> p k m", k=8),
                        ev=f'dtL:{c}:{o}', wait=w, q=f'qdt{o % 2}')

                def ring_w01(c, o):
                    w = []
                    if o >= 2:
                        w.append(f'w0oc:{c}:{o-2}')
                    elif c > 0:
                        w.append(f'w0oc:{c-1}:{o+30}')
                    if c == 0:
                        w += [f'cvS:k0:{(o // 8) * 8 + t}' for t in (4, 5, 6, 7)]
                    dma(rw0[:, o % 2, :, :],
                        w0_s[o, :, :].rearrange("p (k m) -> p k m", k=8),
                        ev=f'w0L:{c}:{o}', wait=w, q=f'qw0{o % 2}')

                def ring_w21(c, bi):
                    h_, k_ = divmod(bi * 2, 32)
                    w = []
                    if bi >= 2:
                        w.append(f'w2b:{c}:{bi-2}')
                    elif c > 0:
                        w.append(f'w2b:{c-1}:{bi+30}')
                    if c == 0:
                        w.append(f'cvS:k2:{(k_ // 2) * 2}')
                        w.append(f'cvS:k2:{(k_ // 2) * 2 + 1}')
                        if bi < 2:
                            w.append(f'cvS:wout:{4 + bi}')
                    sl = (bi * 2) % 4
                    dma(rw2[:, sl:sl + 2, :],
                        w2_s[h_, :, k_ * 512:(k_ + 2) * 512]
                        .rearrange("p (k o) -> p k o", k=2),
                        ev=f'w2L:{c}:{bi}', wait=w, q=f'qw2{bi % 2}')

                def ring_wout1(c, bi):
                    h_, k_ = divmod(bi * 2, 8)
                    w = []
                    if bi >= 2:
                        w.append(f'wob:{c}:{bi-2}')
                    elif c > 0:
                        w.append(f'wob:{c-1}:{bi+6}')
                    if c == 0:
                        w.append(f'cvS:wout:{bi}')
                        if bi < 2:
                            w.append(f'cvS:wout:{6 + bi}')
                    sl = (bi * 2) % 4
                    dma(rout[:, sl:sl + 2, :],
                        wout_s[h_, :, k_ * 512:(k_ + 2) * 512]
                        .rearrange("p (k o) -> p k o", k=2),
                        ev=f'woL:{c}:{bi}', wait=w, q=f'qwo{bi % 2}')

                for c in range(NCHUNK):
                    chunk_zipper(c)

            @block.tensor
            def _(tensor):
                def mm(bank, lhsT, rhs, start, stop, ev=None, wait=()):
                    b.wts(tensor, wait)
                    ins = tensor.matmul(pm[bank][0:lhsT.shape[-1], 0:rhs.shape[-1]],
                                        lhsT, rhs, start=start, stop=stop)
                    if ev:
                        b.inc(ins, 'pe', 1, ev=ev)

                def bcmm(c, n):
                    bkB = 4 + (n % 2) * 2
                    for kind, bk in ((0, bkB), (1, bkB + 1)):
                        w = [f'mrhs{"B" if kind == 0 else "C"}:{c}:{n}']
                        if c == 0 and n == 0 and kind == 0:
                            w.append('ones32R')
                        if bank_r[bk] is not None:
                            w.append(bank_r[bk])
                        mm(bk, ones32[:, :],
                           mrhs[:, kind, :], True, True,
                           ev=f'bcmm{"B" if kind == 0 else "C"}:{c}:{n}',
                           wait=w)
                        bank_w[bk] = f'bcmm{"B" if kind == 0 else "C"}:{c}:{n}'
                        bank_r[bk] = f'bcev:{c}:{n}'

                for c in range(NCHUNK):
                    for tc in range(NTC):
                        bk = tc % 2
                        waits = [f'xn32:{c}:{tc}']
                        for bb in (bk, bk + 2):
                            if bank_r[bb] is not None:
                                waits.append(bank_r[bb])
                        if c == 0 and tc == 0:
                            waits.append('identR')
                        for h in range(8):
                            b.wts(tensor, waits)
                            waits = []
                            ins = tensor.transpose(
                                pm[bk + (0 if h < 4 else 2)]
                                [:, (h % 4) * 128:(h % 4 + 1) * 128],
                                zb[:, tc % 2, h * 128:(h + 1) * 128],
                                ident[:, :])
                            if h == 7:
                                b.inc(ins, 'pe', 1, ev=f'tpg:{c}:{tc}')
                        bank_w[bk] = bank_w[bk + 2] = f'tpg:{c}:{tc}'
                        bank_r[bk] = bank_r[bk + 2] = f'tpe:{c}:{tc}'

                    for g in range(NG):
                        bk = g % 4
                        waits = [f'winL:{c}:{g}']
                        if g == 0:
                            waits.append(f'tpe:{c}:3')
                        if bank_r[bk] is not None:
                            waits.append(bank_r[bk])
                        for k in range(8):
                            mm(bk, rin[:, g % 2, k, :], fybuf[:, k, :],
                               k == 0, k == 7,
                               ev=[f'ipg:{c}:{g}', f'winoc:{c}:{g}'] if k == 7 else None,
                               wait=waits if k == 0 else ())
                            waits = []
                        bank_w[bk] = f'ipg:{c}:{g}'
                        bank_r[bk] = f'ipe:{c}:{g}'
                    for g in range(NG):
                        o = 8 + g
                        bk = g % 4
                        waits = [f'winL:{c}:{o}']
                        if bank_r[bk] is not None:
                            waits.append(bank_r[bk])
                        for k in range(8):
                            mm(bk, rin[:, o % 2, k, :], fybuf[:, k, :],
                               k == 0, k == 7,
                               ev=[f'gpg:{c}:{g}', f'winoc:{c}:{o}'] if k == 7 else None,
                               wait=waits if k == 0 else ())
                            waits = []
                        bank_w[bk] = f'gpg:{c}:{g}'
                        bank_r[bk] = f'gpe:{c}:{g}'
                    bk = 0
                    waits = [f'xcbd:{c}:{NG-1}']
                    if c == 0:
                        waits.append('ld_rbc')
                    if bank_r[bk] is not None:
                        waits.append(bank_r[bk])
                    for k in range(8):
                        mm(bk, rbc[:, k, :], xcb[:, k, :], k == 0, k == 7,
                           ev=f'bcg:{c}' if k == 7 else None,
                           wait=waits if k == 0 else ())
                        waits = []
                    bank_w[0] = f'bcg:{c}'
                    bank_r[0] = f'bce:{c}'
                    for g in range(NG):
                        bk = 1 + g % 3
                        waits = [f'dtL:{c}:{g}']
                        if bank_r[bk] is not None:
                            waits.append(bank_r[bk])
                        for k in range(8):
                            mm(bk, rdt[:, g % 2, k, :], xcb[:, k, :],
                               k == 0, k == 7,
                               ev=[f'dtg:{c}:{g}', f'dtoc:{c}:{g}'] if k == 7 else None,
                               wait=waits if k == 0 else ())
                            waits = []
                        bank_w[bk] = f'dtg:{c}:{g}'
                        bank_r[bk] = f'dte:{c}:{g}'
                    for n_ in range(4):
                        bcmm(c, n_)
                    for _pos, kind_, idx_ in ffn_schedule():
                        if kind_ == 'bc':
                            bcmm(c, idx_)
                            continue
                        if kind_ == 'f0':
                            o = idx_
                            bk = o % 4
                            waits = [f'w0L:{c}:{o}']
                            if bank_r[bk] is not None:
                                waits.append(bank_r[bk])
                            for k in range(8):
                                mm(bk, rw0[:, o % 2, k, :], fybuf[:, k, :],
                                   k == 0, k == 7,
                                   ev=[f'f0g:{c}:{o}', f'w0oc:{c}:{o}'] if k == 7 else None,
                                   wait=waits if k == 0 else ())
                                waits = []
                            bank_w[bk] = f'f0g:{c}:{o}'
                            bank_r[bk] = f'f0e:{c}:{o}'
                            if o == 31:
                                b.mark('pe', f'ffn0done:{c}')
                            continue
                        o = idx_
                        bk = o % 4
                        for k in range(32):
                            ti = o * 32 + k
                            bi = ti // W1_BATCH
                            w = [f'h1e:{c}:{k}'] if o == 0 else []
                            if ti % W1_BATCH == 0:
                                w.append(f'w1L:{c}:{bi}')
                            if k == 0 and bank_r[bk] is not None:
                                w.append(bank_r[bk])
                            ev = []
                            if k == 31:
                                ev.append(f'f1g:{c}:{o}')
                            if (ti + 1) % W1_BATCH == 0:
                                ev.append(f'w1b:{c}:{bi}')
                            mm(bk, rw1[:, ti % W1_RING, :], h1[:, k, :],
                               k == 0, k == 31, ev=ev or None, wait=w)
                        bank_w[bk] = f'f1g:{c}:{o}'
                        bank_r[bk] = f'f1e:{c}:{o}'
                    b.mark('pe', f'ffn1done:{c}')
                    for h in range(2):
                        for k in range(32):
                            hk = h * 32 + k
                            for tc in range(NTC):
                                bk = 4 + tc
                                w = []
                                if tc == 0:
                                    if h == 0:
                                        w.append(f'h2e:{c}:{k}')
                                    if hk % 2 == 0:
                                        w.append(f'w2L:{c}:{hk // 2}')
                                if k == 0 and bank_r[bk] is not None:
                                    w.append(bank_r[bk])
                                ev = []
                                if k == 31:
                                    ev.append(f'f2g:{c}:{h}:{tc}')
                                if tc == 3 and hk % 2 == 1:
                                    ev.append(f'w2b:{c}:{hk // 2}')
                                mm(bk, h2[:, k, tc * 128:(tc + 1) * 128],
                                   rw2[:, hk % 4, :], k == 0, k == 31,
                                   ev=ev or None, wait=w)
                        for tc in range(NTC):
                            bank_w[4 + tc] = f'f2g:{c}:{h}:{tc}'
                            bank_r[4 + tc] = f'f2e:{c}:{h}:{tc}'
                    b.mark('pe', f'ffn2done:{c}')
                    for h in range(2):
                        for k in range(8):
                            hk = h * 8 + k
                            for tc in range(NTC):
                                bk = tc * 2 + h
                                w = []
                                if tc == 0:
                                    if hk == 0:
                                        w.append(f'ybfall:{c}')
                                    if hk % 2 == 0:
                                        w.append(f'woL:{c}:{hk // 2}')
                                if k == 0 and bank_r[bk] is not None:
                                    w.append(bank_r[bk])
                                ev = []
                                if k == 7:
                                    ev.append(f'opg:{c}:{tc}:{h}')
                                if tc == 3 and hk % 2 == 1:
                                    ev.append(f'wob:{c}:{hk // 2}')
                                mm(bk, fybuf[:, k, tc * 128:(tc + 1) * 128],
                                   rout[:, hk % 4, :], k == 0, k == 7,
                                   ev=ev or None, wait=w)
                    for h in range(2):
                        for tc in range(NTC):
                            bank_w[tc * 2 + h] = f'opg:{c}:{tc}:{h}'
                            bank_r[tc * 2 + h] = f'ope:{c}:{tc}'

            @block.scalar
            def _(scalar):
                def act(out, in_, func, ev=None, wait=(), **kw):
                    b.wts(scalar, wait)
                    ins = scalar.activation(out=out, in_=in_, func=func, **kw)
                    if ev:
                        b.inc(ins, 'act', 1, ev=ev)
                    return ins

                def afence(last_ins):
                    b.inc(last_ins, 'act', 1)
                    scalar.wait_ge(b.sems['act'], b.counts['act'])

                act(apos[:, :], cst[:, OA:OA + 128], AF.Exp, ev='aposR',
                    wait=('ld_small',))

                for c in range(NCHUNK):
                    for tc in range(NTC):
                        w = [f'xB:{c}:{tc}']
                        if tc >= 1:
                            w.append(f'ln1v:{c}:{tc-1}')
                        if tc >= 2:
                            w.append(f'tpg:{c}:{tc-2}')
                        elif c > 0:
                            w.append(f'mst:{c-1}:{tc+2}')
                        act(zb[:, tc % 2, :], xst[:, tc % 2, :], AF.Copy,
                            wait=w, accum_out=stats[:, 0:1])
                        act(zb[:, tc % 2, :], xst[:, tc % 2, :], AF.Square,
                            accum_out=stats[:, 1:2], ev=f'ln1s:{c}:{tc}')
                        ins = act(stats[:, 2:3], stats[:, 2:3], AF.Ln,
                                  wait=(f'ln1v:{c}:{tc}',))
                        afence(ins)
                        act(rs_c[:, tc:tc + 1], stats[:, 2:3], AF.Exp,
                            scale=-0.5, ev=f'ln1r:{c}:{tc}')
                    for g in range(NG):
                        w = [f'ipg:{c}:{g}', f'padc:{c}:{g}']
                        if g >= 2:
                            w.append(f'convd:{c}:{g-2}')
                        elif c > 0:
                            w.append(f'convd:{c-1}:{g+6}')
                        act(xssm[:, g % 2, 3:515], pm[g % 4][:, :], AF.Identity,
                            bias=cst[:, OB + g:OB + g + 1], wait=w, ev=f'ipe:{c}:{g}')
                        w = [f'conv4:{c}:{g}']
                        if c > 0:
                            w += [f'dtg:{c-1}:{NG-1}', f'dtxd:{c-1}:{g}']
                        act(xcb[:, g, :], convacc[:, g % 2, :], AF.Silu,
                            bias=cst[:, OCB + g:OCB + g + 1], wait=w, ev=f'xcbd:{c}:{g}')
                    for g in range(NG):
                        w = [f'gpg:{c}:{g}']
                        if c > 0:
                            w.append(f'ybf:{c-1}:{g}')
                        act(gateb[:, g, :], pm[g % 4][:, :], AF.Silu,
                            bias=cst[:, OB + 8 + g:OB + 9 + g], wait=w, ev=f'gpe:{c}:{g}')
                    w = [f'bcg:{c}']
                    if c == 0:
                        w.append('bccolR')
                    if c >= 2:
                        w.append(f'mrhsd:{c-2}:{2*NN-1}')
                    act(bcst[:, c % 2, :], pm[0][0:32, :], AF.Identity,
                        scale=bcsc[0:32, :], bias=cst[0:32, OBC:OBC + 1],
                        wait=w, ev=f'bce:{c}')
                    for g in range(NG):
                        w = [f'dtg:{c}:{g}']
                        if c == 0 and g == 0:
                            w.append('ndtbR')
                        act(sdt[:, :], pm[1 + g % 3][:, :], AF.Sigmoid,
                            scale=-1.0, bias=cst[:, ONDT + g:ONDT + g + 1], wait=w)
                        w = []
                        if c > 0:
                            w.append(f'dtxd:{c-1}:{g}')
                        act(lbf[:, g, :], sdt[:, :], AF.Ln, wait=w,
                            ev=[f'dte:{c}:{g}', f'lbfd:{c}:{g}'])
                    def bc_evac(n):
                        w = [f'bcmmC:{c}:{n}']
                        if n >= 3:
                            w += [f'bcuseB:{c}:{n-3}', f'bcuseC:{c}:{n-3}']
                        elif c > 0:
                            prev = 15 if n == 0 else 12 + n
                            w += [f'bcuseB:{c-1}:{prev}',
                                  f'bcuseC:{c-1}:{prev}']
                        bk = 4 + (n % 2) * 2
                        act(negb[:, n % 3, :], pm[bk][:, :], AF.Identity,
                            wait=w)
                        act(cbr[:, n % 3, :], pm[bk + 1][:, :], AF.Identity,
                            ev=[f'bcL:{c}:{n}', f'bcev:{c}:{n}'])

                    f0pos = {4 + o * 2: o for o in range(32)}
                    f1pos = {70 + (o * 18) // 10: o for o in range(32)}
                    bc_evac(0)
                    bc_evac(1)
                    bc_evac(2)
                    bcpos = {max(1, 8 * n - 6): n for n in range(3, NN)}
                    for i in range(128):
                        n, g = divmod(i, NG)
                        w = []
                        if c == 0 and i == 0:
                            w.append('aposR')
                        if n == 0:
                            w.append(f'lbfd:{c}:{g}')
                        if i >= 2:
                            w.append(f'scank:{c}:{i-2}')
                        elif c > 0:
                            w.append(f'scank:{c-1}:{126+i}')
                        act(da[:, i % 2, :], lbf[:, g, :], AF.Exp,
                            scale=apos[:, g * 16 + n:g * 16 + n + 1],
                            wait=w, ev=f'dA:{c}:{i}')
                        if i in bcpos:
                            bc_evac(bcpos[i])
                        if i in f0pos:
                            o = f0pos[i]
                            w = [f'f0g:{c}:{o}']
                            if c == 0 and o == 0:
                                w.append('scolR')
                            if c > 0:
                                w.append(f'ffn1done:{c-1}')
                            act(h1[:, o, :], pm[o % 4][:, :], AF.Relu,
                                scale=scol[:, 0:1], wait=w,
                                ev=[f'f0e:{c}:{o}', f'h1e:{c}:{o}'])
                        if i in f1pos:
                            o = f1pos[i]
                            w = [f'f1g:{c}:{o}']
                            if c > 0:
                                w.append(f'ffn2done:{c-1}')
                            act(h2[:, o, :], pm[o % 4][:, :], AF.Relu,
                                scale=scol[:, 1:2], wait=w,
                                ev=[f'f1e:{c}:{o}', f'h2e:{c}:{o}'])
                    for h in range(2):
                        for tc in range(NTC):
                            w = [f'f2g:{c}:{h}:{tc}']
                            if h == 0 and c > 0:
                                w.append(f'mcp:{c-1}:{tc}')
                            act(baseb[:, tc * 2 + h, :], pm[4 + tc][:, :],
                                AF.Copy, scale=scol[:, 2:3], wait=w,
                                ev=f'f2e:{c}:{h}:{tc}')
                    for tc in range(NTC):
                        w = [f'zrdy:{c}:{tc}']
                        if tc >= 1:
                            w.append(f'ln2v:{c}:{tc-1}')
                        act(sdt[:, :], zb[:, tc % 2, 0:512], AF.Copy,
                            accum_out=stats[:, 3:4], wait=w)
                        act(sdt[:, :], zb[:, tc % 2, 512:1024], AF.Copy,
                            accum_out=stats[:, 4:5])
                        act(sdt[:, :], zb[:, tc % 2, 0:512], AF.Square,
                            accum_out=stats[:, 5:6])
                        act(sdt[:, :], zb[:, tc % 2, 512:1024], AF.Square,
                            accum_out=stats[:, 6:7], ev=f'ln2s:{c}:{tc}')
                        ins = act(stats[:, 8:9], stats[:, 7:8], AF.Ln,
                                  wait=(f'ln2v:{c}:{tc}',))
                        afence(ins)
                        act(stats[:, 9:10], stats[:, 8:9], AF.Exp, scale=-0.5,
                            ev=f'ln2r:{c}:{tc}')

            @block.vector
            def _(vector):
                vector = EngWrap(vector)

                def w_(*evs):
                    b.wts(vector, [e for e in evs if e])

                def tick(ev):
                    b.inc(vector.last, 'dve', 1, ev=ev)

                def fence():
                    b.inc(vector.last, 'dve', 1)
                    vector.wait_ge(b.sems['dve'], b.counts['dve'])

                w_('ld_small')
                vector.tensor_scalar(out=cst[:, ONDT:ONDT + 8],
                                     in0=cst[:, ONDT:ONDT + 8],
                                     scalar1=-1.0, scalar2=None, op0=ALU.mult)
                tick('ndtbR')
                vector.tensor_scalar(out=scol[:, :], in0=cst[:, OS:OS + 3],
                                     scalar1=2.0, scalar2=None, op0=ALU.mult)
                tick('scolR')
                vector.memset(bcsc[0:32, :], 1.0)
                fence()
                vector.tensor_scalar(out=bcsc[0:16, :], in0=bcsc[0:16, :],
                                     scalar1=-1.0, scalar2=None, op0=ALU.mult)
                vector.tensor_scalar(out=cst[0:16, OBC:OBC + 1],
                                     in0=cst[0:16, OBC:OBC + 1],
                                     scalar1=-1.0, scalar2=None, op0=ALU.mult)
                tick('bccolR')
                w_('ld_wbc')
                vector.tensor_copy(out=rbc[:, :, :],
                                   in_=sdt[:, 0:256]
                                   .rearrange("p (g m) -> p g m", g=8))
                tick('ld_rbc')

                def kconv_range(q0, q1):
                    for q in range(q0, q1):
                        fam, i = KLIST[q]
                        ws = []
                        if q >= 4:
                            pf, pi = KLIST[q - 4]
                            ws.append(f'cvS:{pf}:{pi}')
                        if fam == 'k0':
                            ws += [f'cvL0:k0:{i}', f'cvL1:k0:{i}']
                            w_(*ws)
                            vector.tensor_tensor(
                                out=cb2_slot(q % 4),
                                in0=cu8[:, (2 * i) % 4, :],
                                in1=cu8[:, (2 * i + 1) % 4, :],
                                op=ALU.subtract)
                        else:
                            ws.append(f'cvL:{fam}:{i}')
                            w_(*ws)
                            vector.tensor_scalar(out=cb2_slot(q % 4),
                                                 in0=cu8[:, i % 4, :],
                                                 scalar1=2.0, scalar2=-1.0,
                                                 op0=ALU.mult, op1=ALU.add)
                        tick(f'cvD:{fam}:{i}')

                kconv_range(0, 32)

                for c in range(NCHUNK):
                    for tc in range(NTC):
                        w_(f'ln1s:{c}:{tc}')
                        vector.tensor_scalar(out=mu_c[:, tc:tc + 1],
                                             in0=stats[:, 0:1],
                                             scalar1=1.0 / D, scalar2=None,
                                             op0=ALU.mult)
                        vector.tensor_scalar(out=stats[:, 2:3],
                                             in0=stats[:, 1:2],
                                             scalar1=1.0 / D, scalar2=None,
                                             op0=ALU.mult)
                        fence()
                        vector.tensor_scalar(out=stats[:, 1:2],
                                             in0=mu_c[:, tc:tc + 1],
                                             scalar1=mu_c[:, tc:tc + 1],
                                             scalar2=None, op0=ALU.mult)
                        fence()
                        vector.tensor_sub(out=stats[:, 2:3], in0=stats[:, 2:3],
                                          in1=stats[:, 1:2])
                        fence()
                        vector.tensor_scalar(out=stats[:, 2:3],
                                             in0=stats[:, 2:3],
                                             scalar1=float(EPS), scalar2=None,
                                             op0=ALU.add)
                        tick(f'ln1v:{c}:{tc}')
                        w_(f'ln1r:{c}:{tc}')
                        vector.tensor_scalar(
                            out=zb[:, tc % 2, :], in0=xst[:, tc % 2, :],
                            scalar1=mu_c[:, tc:tc + 1],
                            scalar2=rs_c[:, tc:tc + 1],
                            op0=ALU.subtract, op1=ALU.mult)
                        tick([f'xn32:{c}:{tc}', f'xBc:{c}:{tc}'])
                    for tc in range(NTC):
                        w_(f'tpg:{c}:{tc}',
                           f'opg:{c-1}:3:1' if (c > 0 and tc == 0) else None)
                        for h in range(8):
                            bk = (tc % 2) + (0 if h < 4 else 2)
                            vector.tensor_copy(
                                out=fybuf[:, h, tc * 128:(tc + 1) * 128],
                                in_=pm[bk][:, (h % 4) * 128:(h % 4 + 1) * 128])
                        tick(f'tpe:{c}:{tc}')
                    for g in range(NG):
                        if c % 4 == 0:
                            w_(f'convd:{c-1}:{g+6}' if (c > 0 and g < 2) else None)
                            vector.memset(xssm[:, g % 2, 0:3], 0.0)
                            tick(f'padc:{c}:{g}')
                        else:
                            w_(f'pads:{c-1}:{g}')
                            vector.tensor_copy(out=xssm[:, g % 2, 0:3],
                                               in_=pads[:, g, :])
                            tick(f'padc:{c}:{g}')
                        w_(f'ipe:{c}:{g}')
                        if (c + 1) % 4 != 0:
                            vector.tensor_copy(out=pads[:, g, :],
                                               in_=xssm[:, g % 2, 512:515])
                            tick(f'pads:{c}:{g}')
                        w_(f'xcbd:{c}:{g-2}' if g >= 2 else None,
                           f'padc:{c}:{g}')
                        vector.tensor_scalar(
                            out=convacc[:, g % 2, :],
                            in0=xssm[:, g % 2, 0:512],
                            scalar1=cst[:, OCW + 4 * g:OCW + 4 * g + 1],
                            scalar2=None, op0=ALU.mult)
                        for kk in range(1, 4):
                            vector.scalar_tensor_tensor(
                                out=convacc[:, g % 2, :],
                                in0=xssm[:, g % 2, kk:kk + 512],
                                scalar=cst[:, OCW + 4 * g + kk:OCW + 4 * g + kk + 1],
                                in1=convacc[:, g % 2, :],
                                op0=ALU.mult, op1=ALU.add)
                        tick([f'conv4:{c}:{g}', f'convd:{c}:{g}'])
                        w_(f'xcbd:{c}:{g}',
                           f'yadd:{c-1}:127' if (c > 0 and g == 0) else None)
                        vector.tensor_scalar(out=yacc[:, g, :],
                                             in0=xcb[:, g, :],
                                             scalar1=cst[:, OD + g:OD + g + 1],
                                             scalar2=None, op0=ALU.mult)
                        tick(f'yini:{c}:{g}')
                    for g in range(NG):
                        w_(f'lbfd:{c}:{g}')
                        vector.tensor_mul(out=dtxb[:, g, :], in0=xcb[:, g, :],
                                          in1=lbf[:, g, :])
                        tick(f'dtxd:{c}:{g}')
                    if c == 0:
                        kconv_range(32, len(KLIST))
                    def gbuild(n):
                        for kind, row in ((0, n), (1, 16 + n)):
                            w_(f'bce:{c}' if (n == 0 and kind == 0) else None,
                               f'bcmm{"B" if kind == 0 else "C"}:{c}:{n-1}'
                               if n >= 1 else None)
                            vector.tensor_scalar(
                                out=mrhs[:, kind, :],
                                in0=bcst[:, c % 2, :],
                                scalar1=ident[0:32, row:row + 1],
                                scalar2=None, op0=ALU.mult)
                            evs = [f'mrhs{"B" if kind == 0 else "C"}:{c}:{n}']
                            if n == NN - 1 and kind == 1:
                                evs.append(f'mrhsd:{c}:{2*NN-1}')
                            b.inc(vector.last, 'dve', 1, ev=evs)

                    for n0 in range(4):
                        gbuild(n0)
                    for i in range(128):
                        n, g = divmod(i, NG)
                        if i >= 10 and (i - 10) % 8 == 0 and (i - 10) // 8 + 4 < NN:
                            gbuild((i - 10) // 8 + 4)
                        w_(f'bcL:{c}:{n}' if g == 0 else None,
                           f'dtxd:{c}:{g}' if n == 0 else None)
                        vector.tensor_mul(
                            out=ub[:, :],
                            in0=dtxb[:, g, :], in1=negb[:, n % 3, :])
                        if g == NG - 1:
                            tick(f'bcuseB:{c}:{n}')
                        w_(f'dA:{c}:{i}',
                           f'ymul:{c}:{i-2}' if i >= 2 else
                           (f'ymul:{c-1}:{126+i}' if c > 0 else None),
                           f'carrydone:{c-1}' if (i == 0 and c % 4 != 0) else None,
                           f'hcclear:{c}' if (i == 0 and c % 4 == 0) else None)
                        ins = vector.tensor_tensor_scan(
                            out=hb[:, i % 2, :], data0=da[:, i % 2, :],
                            data1=ub[:, :],
                            initial=hc[:, g * 16 + n:g * 16 + n + 1],
                            op0=ALU.mult, op1=ALU.add)
                        b.inc(ins, 'dve', 1, ev=f'scank:{c}:{i}')
                        if i >= 1:
                            w_(f'ymul:{c}:{i-1}')
                            gp = (i - 1) % NG
                            vector.tensor_add(out=yacc[:, gp, :],
                                              in0=yacc[:, gp, :],
                                              in1=pb[:, (i - 1) % 2, :])
                            tick(f'yadd:{c}:{i-1}')
                    w_(f'ymul:{c}:127')
                    vector.tensor_add(out=yacc[:, 7, :], in0=yacc[:, 7, :],
                                      in1=pb[:, 1, :])
                    tick(f'yadd:{c}:127')
                    for g in range(NG):
                        w_(f'ffn0done:{c}' if g == 0 else None)
                        vector.tensor_mul(out=fybuf[:, g, :],
                                          in0=yacc[:, g, :],
                                          in1=gateb[:, g, :])
                        tick([f'ybf:{c}:{g}'] +
                             ([f'ybfall:{c}'] if g == NG - 1 else []))
                    for tc in range(NTC):
                        w_(f'xE:{c}:{tc}',
                           f'tpg:{c}:{tc+2}' if tc < 2 else None,
                           f'mst:{c}:{tc-2}' if tc >= 2 else None)
                        vector.tensor_scalar(
                            out=zb[:, tc % 2, :], in0=xst[:, tc % 2, :],
                            scalar1=mu_c[:, tc:tc + 1],
                            scalar2=rs_c[:, tc:tc + 1],
                            op0=ALU.subtract, op1=ALU.mult)
                        fence()
                        if c == 0 and tc == 0:
                            w_('ld_reps')
                        for h in range(2):
                            w_(f'opg:{c}:{tc}:{h}')
                            vector.tensor_add(
                                out=zb[:, tc % 2, h * 512:(h + 1) * 512],
                                in0=zb[:, tc % 2, h * 512:(h + 1) * 512],
                                in1=pm[tc * 2 + h][:, :])
                            fence()
                        vector.tensor_add(out=zb[:, tc % 2, :],
                                          in0=zb[:, tc % 2, :],
                                          in1=outbr[:, :])
                        tick([f'zrdy:{c}:{tc}', f'ope:{c}:{tc}'])
                        w_(f'ln2s:{c}:{tc}')
                        vector.tensor_add(out=stats[:, 3:4], in0=stats[:, 3:4],
                                          in1=stats[:, 4:5])
                        vector.tensor_add(out=stats[:, 5:6], in0=stats[:, 5:6],
                                          in1=stats[:, 6:7])
                        fence()
                        vector.tensor_scalar(out=stats[:, 3:4],
                                             in0=stats[:, 3:4],
                                             scalar1=1.0 / D, scalar2=None,
                                             op0=ALU.mult)
                        vector.tensor_scalar(out=stats[:, 5:6],
                                             in0=stats[:, 5:6],
                                             scalar1=1.0 / D, scalar2=None,
                                             op0=ALU.mult)
                        fence()
                        vector.tensor_scalar(out=stats[:, 4:5],
                                             in0=stats[:, 3:4],
                                             scalar1=stats[:, 3:4],
                                             scalar2=None, op0=ALU.mult)
                        fence()
                        vector.tensor_sub(out=stats[:, 7:8], in0=stats[:, 5:6],
                                          in1=stats[:, 4:5])
                        fence()
                        vector.tensor_scalar(out=stats[:, 7:8],
                                             in0=stats[:, 7:8],
                                             scalar1=float(EPS), scalar2=None,
                                             op0=ALU.add)
                        tick(f'ln2v:{c}:{tc}')
                        w_(f'ln2r:{c}:{tc}')
                        vector.tensor_scalar(
                            out=zb[:, tc % 2, :], in0=zb[:, tc % 2, :],
                            scalar1=stats[:, 3:4], scalar2=stats[:, 9:10],
                            op0=ALU.subtract, op1=ALU.mult)
                        fence()
                        vector.tensor_mul(out=zb[:, tc % 2, :],
                                          in0=zb[:, tc % 2, :],
                                          in1=lngr[:, :])
                        fence()
                        vector.tensor_add(out=zb[:, tc % 2, :],
                                          in0=zb[:, tc % 2, :],
                                          in1=lnbr[:, :])
                        fence()
                        vector.tensor_add(out=zb[:, tc % 2, :],
                                          in0=zb[:, tc % 2, :],
                                          in1=xst[:, tc % 2, :])
                        tick([f'zdone:{c}:{tc}', f'xEc:{c}:{tc}'])
                if DEBUG_DUMP and c == NCHUNK - 1:
                    for bk in range(8):
                        if bk > 0:
                            b.wt(vector, f'dpmst:{bk-1}')
                        vector.tensor_copy(out=sdt[:, :], in_=pm[bk][:, :])
                        tick(f'dpm:{bk}')

            @block.gpsimd
            def _(gpsimd):
                gpsimd = EngWrap(gpsimd)

                def w_(*evs):
                    b.wts(gpsimd, [e for e in evs if e])

                def tick(ev):
                    b.inc(gpsimd.last, 'gps', 1, ev=ev)

                gpsimd.memset(ident[:, :], 0.0)
                b.inc(gpsimd.last, 'gps', 1)
                gpsimd.wait_ge(b.sems['gps'], b.counts['gps'])
                gpsimd.affine_select(out=ident[:, :], in_=ident[:, :],
                                     compare_op=ALU.not_equal, fill=1.0,
                                     base=0, pattern=[[-1, 128]],
                                     channel_multiplier=1)
                tick('identR')
                gpsimd.memset(hc[:, :], 0.0)
                tick('hcclear:0')
                gpsimd.memset(ones32[:, :], 1.0)
                tick('ones32R')
                b.inc(gpsimd.dma_start(out=outbr[:, :], in_=outb_h), 'qr', 16)
                b.inc(gpsimd.dma_start(out=lngr[:, :], in_=lng_h), 'qr', 16)
                b.inc(gpsimd.dma_start(out=lnbr[:, :], in_=lnb_h), 'qr', 16)
                b.mark('qr', 'ld_reps')

                qg_pieces = []
                for j in range(16):
                    qg_pieces.append((win_h[:, j, :, :]
                                      .rearrange("p k m -> p (k m)"),
                                      win_s[j, :, :], f'cvS:win:{j}'))
                for j in range(8):
                    qg_pieces.append((wdt_h[:, j, :, :]
                                      .rearrange("p k m -> p (k m)"),
                                      wdt_s[j, :, :], f'cvS:wdt:{j}'))
                for j in range(8):
                    h_, k_ = divmod(2 * j, 8)
                    qg_pieces.append((wout_h
                                      .rearrange("p h k o -> p (h k o)")
                                      [:, j * 1024:(j + 1) * 1024],
                                      wout_s[h_, :, k_ * 512:(k_ + 2) * 512],
                                      f'cvS:wout:{j}'))
                NQG = len(qg_pieces)

                for g in (0, 1):
                    b.inc(gpsimd.dma_start(out=cbf_slot(g % 4),
                                           in_=qg_pieces[g][0]),
                          f'qgl{g % 4}', 16, ev=f'cvLg:{g}')
                for g in range(NQG):
                    if g + 2 < NQG:
                        w_(qg_pieces[g - 2][2] if g >= 2 else None)
                        b.inc(gpsimd.dma_start(out=cbf_slot((g + 2) % 4),
                                               in_=qg_pieces[g + 2][0]),
                              f'qgl{(g + 2) % 4}', 16, ev=f'cvLg:{g+2}')
                    w_(f'cvLg:{g}')
                    b.inc(gpsimd.dma_start(out=qg_pieces[g][1],
                                           in_=cbf_slot(g % 4)),
                          f'qgs{g % 4}', 16, ev=qg_pieces[g][2])

                for c in range(NCHUNK):
                    s, j0 = c // 4, (c % 4)
                    tok0 = j0 * TC
                    if c == 4:
                        w_(f'scank:3:127')
                        gpsimd.memset(hc[:, :], 0.0)
                        tick('hcclear:4')
                    for i in range(128):
                        n, g = divmod(i, NG)
                        w_(f'scank:{c}:{i}',
                           f'yadd:{c}:{i-2}' if i >= 2 else
                           (f'yadd:{c-1}:{126+i}' if c > 0 else None))
                        gpsimd.tensor_mul(out=pb[:, i % 2, :],
                                          in0=hb[:, i % 2, :],
                                          in1=cbr[:, n % 3, :])
                        ins = gpsimd.tensor_copy(
                            out=hc[:, g * 16 + n:g * 16 + n + 1],
                            in_=hb[:, i % 2, 511:512])
                        evs = [f'ymul:{c}:{i}']
                        if g == NG - 1:
                            evs.append(f'bcuseC:{c}:{n}')
                        if i == 127:
                            evs.append(f'carrydone:{c}')
                        b.inc(ins, 'gps', 1, ev=evs)
                    for tc in range(NTC):
                        w_(f'zdone:{c}:{tc}', f'f2e:{c}:{1}:{tc}')
                        gpsimd.tensor_tensor(
                            out=zb[:, tc % 2, :], in0=zb[:, tc % 2, :],
                            in1=baseb[:, tc * 2:tc * 2 + 2, :]
                            .rearrange("p a t -> p (a t)"),
                            op=ALU.add)
                        b.inc(gpsimd.last, 'gps', 1, ev=f'mcp:{c}:{tc}')
                        t0 = tok0 + tc * 128
                        b.inc(gpsimd.dma_start(
                            out=out_hbm[s, t0:t0 + 128, :],
                            in_=zb[:, tc % 2, :]), f'qst{tc % 2}', 16,
                            ev=f'mst:{c}:{tc}')

                if DEBUG_DUMP:
                    for bk in range(8):
                        b.wt(gpsimd, f'dpm:{bk}')
                        b.inc(gpsimd.dma_start(
                            out=dbg["d_pm"][:, bk, :], in_=sdt[:, :]), 'qst0', 16,
                            ev=f'dpmst:{bk}')
                    for nm, sbuf in [("d_xcb", xcb), ("d_lbf", lbf),
                                     ("d_gate", gateb), ("d_fy", fybuf),
                                     ("d_yacc", yacc), ("d_h1", h1),
                                     ("d_h2", h2), ("d_dtx", dtxb),
                                     ("d_zb", zb), ("d_negb", negb),
                                     ("d_outbr", outbr), ("d_lngr", lngr),
                                     ("d_lnbr", lnbr), ("d_base", baseb),
                                     ("d_rout", rout), ("d_hc", hc),
                                     ("d_xst", xst), ("d_rw1", rw1),
                                     ("d_rw0", rw0),
                                     ("d_cbr", cbr), ("d_stats", stats),
                                     ("d_mu", mu_c), ("d_rs", rs_c)]:
                        b.inc(gpsimd.dma_start(out=dbg[nm],
                                               in_=sbuf[...]), 'qst0', 16)
                for s_ in ('qst0', 'qst1'):
                    gpsimd.wait_ge(b.sems[s_], b.counts[s_])

    return nc, b


_CACHE = {}


def _get_program():
    if 'nc' not in _CACHE:
        _, bA = build_program(None)
        nc, _ = build_program(bA.events)
        _CACHE['nc'] = nc
    return _CACHE['nc']


def make_inputs(inputs):
    x = np.ascontiguousarray(np.asarray(inputs['L0_attn_x'], dtype=np.float32))
    ipw = np.asarray(inputs['L0_attn_in_proj_w'], dtype=np.float32)
    ipb = np.asarray(inputs['L0_attn_in_proj_b'], dtype=np.float32)
    cw = np.asarray(inputs['L0_attn_conv_w'], dtype=np.float32)[:, 0, :]
    cb = np.asarray(inputs['L0_attn_conv_b'], dtype=np.float32)
    bcw = np.asarray(inputs['L0_attn_bc_w'], dtype=np.float32)
    bcb = np.asarray(inputs['L0_attn_bc_b'], dtype=np.float32)
    dtw = np.asarray(inputs['L0_attn_dt_w'], dtype=np.float32)
    dtb = np.asarray(inputs['L0_attn_dt_b'], dtype=np.float32)
    alog = np.asarray(inputs['L0_attn_A_log'], dtype=np.float32)
    Dv = np.asarray(inputs['L0_attn_D'], dtype=np.float32)
    ow = np.asarray(inputs['L0_attn_out_w'], dtype=np.float32)
    ob = np.asarray(inputs['L0_attn_out_b'], dtype=np.float32)
    lng = np.asarray(inputs['L0_attn_ln_g'], dtype=np.float32)
    lnb = np.asarray(inputs['L0_attn_ln_b'], dtype=np.float32)
    k0 = np.asarray(inputs['L0_ffn_k0'])
    k1 = np.asarray(inputs['L0_ffn_k1'])
    k2 = np.asarray(inputs['L0_ffn_k2'])
    s0 = np.asarray(inputs['L0_ffn_s0'], dtype=np.float32)
    s1 = np.asarray(inputs['L0_ffn_s1'], dtype=np.float32)
    s2 = np.asarray(inputs['L0_ffn_s2'], dtype=np.float32)

    win_h = np.ascontiguousarray(
        ipw.reshape(16, 128, 8, 128).transpose(3, 0, 2, 1))
    wdt_h = np.ascontiguousarray(
        dtw.reshape(8, 128, 8, 128).transpose(3, 0, 2, 1))
    wout_h = np.ascontiguousarray(
        ow.reshape(2, 512, 8, 128).transpose(3, 0, 2, 1))
    wbc_h = np.ascontiguousarray(bcw.reshape(32, 8, 128).transpose(2, 1, 0))
    k0_h = np.ascontiguousarray(k0.view(np.uint8).reshape(2, 8, 128, 4096))
    k1_h = np.ascontiguousarray(k1.view(np.uint8).reshape(32, 128, 4096))
    k2_h = np.ascontiguousarray(k2.view(np.uint8).reshape(32, 128, 1024))
    cst_h = np.zeros((128, 204), np.float32)
    cst_h[:, 0:16] = ipb.reshape(16, 128).T
    cst_h[:, 16:48] = cw.reshape(8, 128, 4).transpose(1, 0, 2).reshape(128, 32)
    cst_h[:, 48:56] = cb.reshape(8, 128).T
    cst_h[:, 56:64] = dtb.reshape(8, 128).T
    cst_h[0:32, 64] = bcb
    cst_h[:, 65:73] = Dv.reshape(8, 128).T
    cst_h[:, 73:76] = np.stack([s0, s1, s2])[None, :]
    cst_h[:, 76:204] = alog.reshape(8, 128, 16).transpose(1, 0, 2).reshape(128, 128)
    outb_h = np.ascontiguousarray(np.broadcast_to(ob, (128, 1024)))
    lng_h = np.ascontiguousarray(np.broadcast_to(lng, (128, 1024)))
    lnb_h = np.ascontiguousarray(np.broadcast_to(lnb, (128, 1024)))

    shared = dict(win_h=win_h, wdt_h=wdt_h, wout_h=wout_h, wbc_h=wbc_h,
                  k0_h=k0_h, k1_h=k1_h, k2_h=k2_h, cst_h=cst_h,
                  outb_h=outb_h, lng_h=lng_h, lnb_h=lnb_h)
    in_maps = []
    for i in range(NCORE):
        m = dict(shared)
        m['x_in'] = np.ascontiguousarray(x[2 * i:2 * i + 2])
        in_maps.append(m)

    return in_maps


def kernel(**inputs):
    in_maps = make_inputs(inputs)
    nc = _get_program()
    res = run_bass_kernel_spmd(nc, in_maps, core_ids=list(range(NCORE)))
    out = np.concatenate([r['out'] for r in res.results], axis=0)
    return out.reshape(16, 2048, 1024).astype(np.float32)


# version 20 (final)
# speedup vs baseline: 1.0739x; 1.0126x over previous
import numpy as np
from contextlib import ExitStack

import concourse.bass as bass
import concourse.mybir as mybir
from concourse.bass_utils import run_bass_kernel_spmd

F32 = mybir.dt.float32
BF16 = mybir.dt.bfloat16
U8 = mybir.dt.uint8
AF = mybir.ActivationFunctionType
ALU = mybir.AluOpType

NCORE = 8
BLOC = 2
L = 2048
D = 1024
NCHUNK = 8
TC = 512
NTC = 4
NG = 8
NN = 16
EPS = 1e-5
W1_RING = 32
W1_BATCH = 16
DEBUG_DUMP = False
NO_BCAST = False


class Bld:
    def __init__(self, known):
        self.known = known
        self.events = {}
        names = ['pe', 'act', 'dve', 'gps', 'qs', 'qg', 'qr',
                 'qcv0a', 'qcv0b', 'qcv0c', 'qcv0d',
                 'qcv1a', 'qcv1b', 'qcv1c', 'qcv1d',
                 'qcv2a', 'qcv2b', 'qcv2c', 'qcv2d',
                 'qgl0', 'qgl1', 'qgl2', 'qgl3',
                 'qgs0', 'qgs1', 'qgs2', 'qgs3',
                 'qx0', 'qx1', 'qwin0', 'qwin1', 'qdt0', 'qdt1',
                 'qw00', 'qw01', 'qw10', 'qw11', 'qw12', 'qw13',
                 'qw20', 'qw21', 'qwo0', 'qwo1',
                 'qcu0', 'qcu1', 'qcu2', 'qcu3',
                 'qst0', 'qst1']
        self.counts = {k: 0 for k in names}
        self.sem_names = names
        self.sems = {}

    def inc(self, instr, sem, n=1, ev=None):
        self.counts[sem] += n
        if ev is not None:
            if isinstance(ev, str):
                ev = [ev]
            for e in ev:
                assert e not in self.events, f"duplicate event {e}"
                self.events[e] = (sem, self.counts[sem])
        instr.then_inc(self.sems[sem], n)

    def mark(self, sem, ev):
        if isinstance(ev, str):
            ev = [ev]
        for e in ev:
            assert e not in self.events, f"duplicate event {e}"
            self.events[e] = (sem, self.counts[sem])

    def wt(self, eng, ev):
        if not ev:
            return
        if self.known is not None:
            sem, val = self.known[ev]
            eng.wait_ge(self.sems[sem], val)
        elif ev in self.events:
            sem, val = self.events[ev]
            eng.wait_ge(self.sems[sem], val)

    def wts(self, eng, evs):
        for e in evs:
            self.wt(eng, e)


def ffn_schedule():
    items = []
    for o in range(32):
        items.append((4 + 2 * o, 'f0', o))
    for o in range(32):
        items.append((70 + (o * 18) // 10, 'f1', o))
    for n in range(4, 16):
        items.append((8 * n - 6, 'bc', n))
    items.sort(key=lambda t: (t[0], t[1]))
    return items


KLIST = ([('k0', i) for i in range(32)] +
         [('k1', i) for i in range(128)] +
         [('k2', i) for i in range(32)])


def kmpos(q):
    return 7 * (q // 6) + 1 + q % 6


def qgpos(g):
    return 7 * g


class EngWrap:

    def __init__(self, eng):
        self._eng = eng
        self.last = None

    def __getattr__(self, name):
        f = getattr(self._eng, name)
        if not callable(f):
            return f
        def wrap(*a, **kw):
            r = f(*a, **kw)
            if name not in ('wait_ge', 'then_inc'):
                self.last = r
            return r
        return wrap


def build_program(known):
    nc = bass.Bass("TRN2", target_bir_lowering=False,
                   detect_race_conditions=False)
    b = Bld(known)

    dt_ = nc.dram_tensor
    x_in = dt_("x_in", [BLOC, L, D], F32, kind="ExternalInput").ap()
    win_h = dt_("win_h", [128, 16, 8, 128], F32, kind="ExternalInput").ap()
    wdt_h = dt_("wdt_h", [128, 8, 8, 128], F32, kind="ExternalInput").ap()
    wout_h = dt_("wout_h", [128, 2, 8, 512], F32, kind="ExternalInput").ap()
    wbc_h = dt_("wbc_h", [128, 8, 32], F32, kind="ExternalInput").ap()
    k0_h = dt_("k0_h", [2, 8, 128, 4096], U8, kind="ExternalInput").ap()
    k1_h = dt_("k1_h", [32, 128, 4096], U8, kind="ExternalInput").ap()
    k2_h = dt_("k2_h", [32, 128, 1024], U8, kind="ExternalInput").ap()
    cst_h = dt_("cst_h", [128, 204], F32, kind="ExternalInput").ap()
    outb_h = dt_("outb_h", [128, 1024], F32, kind="ExternalInput").ap()
    lng_h = dt_("lng_h", [128, 1024], F32, kind="ExternalInput").ap()
    lnb_h = dt_("lnb_h", [128, 1024], F32, kind="ExternalInput").ap()
    out_hbm = dt_("out", [BLOC, L, D], F32, kind="ExternalOutput").ap()

    dbg = {}
    if DEBUG_DUMP:
        for nm, shp, dty in [("d_xcb", [128, 8, 512], BF16),
                             ("d_lbf", [128, 8, 512], BF16),
                             ("d_gate", [128, 8, 512], BF16),
                             ("d_fy", [128, 8, 512], BF16),
                             ("d_yacc", [128, 8, 512], BF16),
                             ("d_h1", [128, 32, 512], BF16),
                             ("d_h2", [128, 32, 512], BF16),
                             ("d_dtx", [128, 8, 512], BF16),
                             ("d_zb", [128, 2, 1024], F32),
                             ("d_negb", [128, 4, 512], BF16),
                             ("d_cbr", [128, 4, 512], BF16),
                             ("d_stats", [128, 10], F32),
                             ("d_outbr", [128, 1024], BF16),
                             ("d_lngr", [128, 1024], BF16),
                             ("d_lnbr", [128, 1024], BF16),
                             ("d_base", [128, 8, 512], BF16),
                             ("d_rout", [128, 4, 512], BF16),
                             ("d_hc", [128, 128], F32),
                             ("d_xst", [128, 2, 1024], F32),
                             ("d_pm", [128, 8, 512], F32),
                             ("d_zall", [128, 4, 1024], F32),
                             ("d_ball", [128, 8, 512], F32),
                             ("zlog", [NCHUNK, 4, 128, 1024], F32),
                             ("blog", [NCHUNK, 8, 128, 512], F32),
                             ("d_rw1", [128, 32, 128], BF16),
                             ("d_rw0", [128, 2, 8, 128], BF16),
                             ("d_mu", [128, 4], F32),
                             ("d_rs", [128, 4], F32)]:
            dbg[nm] = dt_(nm, shp, dty, kind="ExternalOutput").ap()
    win_s = dt_("win_s", [16, 128, 1024], BF16).ap()
    wdt_s = dt_("wdt_s", [8, 128, 1024], BF16).ap()
    wout_s = dt_("wout_s", [2, 128, 4096], BF16).ap()
    w0_s = dt_("w0_s", [32, 128, 1024], BF16).ap()
    w1_s = dt_("w1_s", [32, 128, 4096], BF16).ap()
    w2_s = dt_("w2_s", [2, 128, 16384], BF16).ap()

    with ExitStack() as st:
        sb = lambda name, shape, dt: st.enter_context(
            nc.sbuf_tensor(name, shape, dt))
        ps = lambda name, shape, dt: st.enter_context(
            nc.psum_tensor(name, shape, dt))
        for s_ in b.sem_names:
            b.sems[s_] = st.enter_context(nc.semaphore(s_))

        ident = sb("ident", [128, 128], F32)
        apos = sb("apos", [128, 128], F32)
        cst = sb("cst", [128, 204], F32)
        bcsc = sb("bcsc", [32, 1], F32)
        outbr = sb("outbr", [128, 1024], BF16)
        lngr = sb("lngr", [128, 1024], BF16)
        lnbr = sb("lnbr", [128, 1024], BF16)
        scol = sb("scol", [128, 3], F32)
        stats = sb("stats", [128, 10], F32)
        mu_c = sb("mu_c", [128, 4], F32)
        rs_c = sb("rs_c", [128, 4], F32)
        rin = sb("rin", [128, 2, 8, 128], BF16)
        rdt = sb("rdt", [128, 2, 8, 128], BF16)
        rout = sb("rout", [128, 4, 512], BF16)
        rw0 = sb("rw0", [128, 2, 8, 128], BF16)
        rw1 = sb("rw1", [128, W1_RING, 128], BF16)
        rw2 = sb("rw2", [128, 4, 512], BF16)
        rbc = sb("rbc", [128, 8, 32], BF16)
        xcb = sb("xcb", [128, 8, 512], BF16)
        gateb = sb("gateb", [128, 8, 512], BF16)
        lbf = sb("lbf", [128, 8, 512], BF16)
        dtxb = sb("dtxb", [128, 8, 512], BF16)
        yacc = sb("yacc", [128, 8, 512], BF16)
        fybuf = sb("fybuf", [128, 8, 512], BF16)
        hc = sb("hc", [128, 128], F32)
        sdt = sb("sdt", [128, 512], F32)
        convacc = sb("convacc", [128, 2, 512], F32)
        xssm = sb("xssm", [128, 2, 515], F32)
        pads = sb("pads", [128, 8, 3], F32)
        negb = sb("negb", [128, 3, 512], BF16)
        cbr = sb("cbr", [128, 3, 512], BF16)
        da = sb("da", [128, 2, 512], F32)
        ub = sb("ub", [128, 512], BF16)
        hb = sb("hb", [128, 2, 512], F32)
        pb = sb("pb", [128, 2, 512], F32)
        h1 = sb("h1", [128, 32, 512], BF16)
        h2 = sb("h2", [128, 32, 512], BF16)
        xst = sb("xst", [128, 2, 1024], F32)
        zb = sb("zb", [128, 2, 1024], F32)
        baseb = sb("baseb", [128, 8, 512], BF16)
        bcst = sb("bcst", [32, 2, 512], BF16)
        mrhs = sb("mrhs", [32, 2, 512], BF16)
        ones32 = sb("ones32", [32, 128], BF16)
        cu8 = baseb[:, 0:4, :].rearrange("p a t -> p (a t)") \
            .bitcast(U8).rearrange("p (a t) -> p a t", a=4)
        cbfA = rw2[:, :, :].rearrange("p a t -> p (a t)") \
            .rearrange("p (a b) -> p a b", a=2)
        cbfB = rout[:, :, :].rearrange("p a t -> p (a t)") \
            .rearrange("p (a b) -> p a b", a=2)
        cbf_slot = lambda s: (cbfA if s < 2 else cbfB)[:, s % 2, :]
        cb2 = baseb[:, 4:8, :].rearrange("p a t -> p (a t)") \
            .rearrange("p (a b) -> p a b", a=2)
        cb2B = da[:, :, :].rearrange("p a t -> p (a t)") \
            .bitcast(BF16).rearrange("p (a b) -> p a b", a=2)
        cb2_slot = lambda s: (cb2 if s < 2 else cb2B)[:, s % 2, :]

        OB, OCW, OCB, ONDT, OBC, OD, OS, OA = 0, 16, 48, 56, 64, 65, 73, 76
        pm = [ps(f"pm{i}", [128, 512], F32) for i in range(8)]
        bank_w = [None] * 8
        bank_r = [None] * 8

        with nc.Block() as block:

            @block.sync
            def _(sync):
                def dma(dst, src, ev=None, wait=(), q='qs'):
                    if isinstance(wait, str):
                        wait = (wait,)
                    b.wts(sync, [w for w in wait if w])
                    b.inc(sync.dma_start(out=dst, in_=src), q, 16, ev=ev)

                dma(cst[:, :], cst_h, ev='ld_small')
                dma(sdt[:, 0:256], wbc_h.rearrange("p g m -> p (g m)"),
                    ev='ld_wbc', wait='ld_small')

                dma(xst[:, 0, :], x_in[0, 0:128, :], ev='xB:0:0', q='qx0')
                dma(xst[:, 1, :], x_in[0, 128:256, :], ev='xB:0:1', q='qx1')

                def w1_load(c, bi):
                    t0 = bi * W1_BATCH
                    o_chunk, kk = divmod(t0, 32)
                    w = []
                    if bi >= W1_RING // W1_BATCH:
                        w.append(f'w1b:{c}:{bi - W1_RING // W1_BATCH}')
                    elif c > 0:
                        w.append(f'w1b:{c-1}:{bi + 64 - W1_RING // W1_BATCH}')
                    if c == 0:
                        w += [f'cvS:k1:{(o_chunk // 8) * 32 + t}'
                              for t in (28, 29, 30, 31)]
                    sl = t0 % W1_RING
                    dma(rw1[:, sl:sl + W1_BATCH, :],
                        w1_s[o_chunk, :,
                             kk * 128:(kk + W1_BATCH) * 128]
                        .rearrange("p (k m) -> p k m", k=W1_BATCH),
                        ev=f'w1L:{c}:{bi}', wait=w, q=f'qw1{bi % 4}')

                def kload(q):
                    fam, i = KLIST[q]
                    if fam == 'k0':
                        j, k = divmod(i, 8)
                        for half in (0, 1):
                            dma(cu8[:, (2 * i + half) % 4, :],
                                k0_h[half, k, :, j * 1024:(j + 1) * 1024],
                                ev=f'cvL{half}:k0:{i}',
                                wait=(f'cvD:k0:{i-2}',) if i >= 2 else (),
                                q=f'qcu{(2 * i + half) % 4}')
                    elif fam == 'k1':
                        j, k = divmod(i, 32)
                        dma(cu8[:, i % 4, :],
                            k1_h[k, :, j * 1024:(j + 1) * 1024],
                            ev=f'cvL:k1:{i}',
                            wait=(f'cvD:k1:{i-4}',) if i >= 4
                            else (f'cvD:k0:{30 + i // 2}',),
                            q=f'qcu{i % 4}')
                    else:
                        dma(cu8[:, i % 4, :], k2_h[i, :, :],
                            ev=f'cvL:k2:{i}',
                            wait=(f'cvD:k2:{i-4}',) if i >= 4
                            else (f'cvD:k1:{124 + i}',),
                            q=f'qcu{i % 4}')

                def kstore(q):
                    fam, i = KLIST[q]
                    b.wt(sync, f'cvD:{fam}:{i}')
                    if fam == 'k0':
                        j, k = divmod(i, 8)
                        out = w0_s[j * 8:(j + 1) * 8, :,
                                   k * 128:(k + 1) * 128] \
                            .rearrange("o p m -> p o m")
                        in_ = cb2_slot(q % 4).rearrange("p (o m) -> p o m",
                                                        o=8)
                        sem = 'qcv0' + 'abcd'[i % 4]
                    elif fam == 'k1':
                        j, k = divmod(i, 32)
                        out = w1_s[j * 8:(j + 1) * 8, :,
                                   k * 128:(k + 1) * 128] \
                            .rearrange("o p m -> p o m")
                        in_ = cb2_slot(q % 4).rearrange("p (o m) -> p o m",
                                                        o=8)
                        sem = 'qcv1' + 'abcd'[i % 4]
                    else:
                        out = w2_s[:, :, i * 512:(i + 1) * 512] \
                            .rearrange("h p o -> p h o")
                        in_ = cb2_slot(q % 4).rearrange("p (h o) -> p h o",
                                                        h=2)
                        sem = 'qcv2' + 'abcd'[i % 4]
                    b.inc(sync.dma_start(out=out, in_=in_), sem, 16,
                          ev=f'cvS:{fam}:{i}')

                def chunk_zipper(c):
                    s, j0 = c // 4, (c % 4)
                    tok0 = j0 * TC
                    items = []
                    if c > 0:
                        for tc in (0, 1):
                            items.append((-300 + tc, lambda tc=tc: dma(
                                xst[:, tc, :],
                                x_in[s, tok0 + tc * 128:tok0 + (tc + 1) * 128, :],
                                ev=f'xB:{c}:{tc}',
                                wait=(f'xEc:{c-1}:{tc+2}',), q=f'qx{tc}')))
                        for tc in (2, 3):
                            items.append((-298 + tc, lambda tc=tc: dma(
                                xst[:, tc % 2, :],
                                x_in[s, tok0 + tc * 128:tok0 + (tc + 1) * 128, :],
                                ev=f'xB:{c}:{tc}',
                                wait=(f'xBc:{c}:{tc-2}',), q=f'qx{tc % 2}')))
                    else:
                        for tc in (2, 3):
                            items.append((-227.5 + 0.01 * tc, lambda tc=tc: dma(
                                xst[:, tc % 2, :],
                                x_in[0, tc * 128:(tc + 1) * 128, :],
                                ev=f'xB:0:{tc}', wait=(f'xBc:0:{tc-2}',),
                                q=f'qx{tc % 2}')))
                    post = lambda k: -90.0 + (k + 300.0) * 0.01
                    if c == 0:
                        for q in range(len(KLIST)):
                            if q < 32:
                                lk = -250 + 0.7 * max(q - 3, 0)
                                sk = -250 + 0.7 * q + 0.35
                            else:
                                lk = -227.0 + 0.55 * (max(q - 3, 32) - 32)
                                sk = -227.0 + 0.55 * (q - 32) + 0.3
                            items.append((lk, lambda q=q: kload(q)))
                            items.append((sk, lambda q=q: kstore(q)))
                    for o in range(16):
                        items.append(((-249 + 0.01 * o if o < 2
                                       else -227.4 + 0.005 * o) if c == 0
                                      else -200 + o,
                                      lambda o=o: ring_win1(c, o)))
                    for o in range(8):
                        items.append((-227.3 + 0.002 * o if c == 0
                                      else -100 + o,
                                      lambda o=o: ring_dt1(c, o)))
                    for o in range(32):
                        kk0 = post(-10 + 2 * o if o < 2 else 2 * o)
                        items.append((kk0 if c == 0
                                      else (-10 + 2 * o if o < 2 else 2 * o),
                                      lambda o=o: ring_w01(c, o)))
                    for bi in range(64):
                        kk = 30 if bi < 2 else 64 + (bi - 2) * 1.8
                        items.append((post(kk) if c == 0 else kk,
                                      lambda bi=bi: w1_load(c, bi)))
                    for bi in range(32):
                        items.append((post(300 + bi) if c == 0 else 300 + bi,
                                      lambda bi=bi: ring_w21(c, bi)))
                    for bi in range(8):
                        items.append((post(400 + bi) if c == 0 else 400 + bi,
                                      lambda bi=bi: ring_wout1(c, bi)))
                    for tc in range(NTC):
                        items.append((post(500 + tc) if c == 0 else 500 + tc,
                                      lambda tc=tc: dma(
                            xst[:, tc % 2, :],
                            x_in[s, tok0 + tc * 128:tok0 + (tc + 1) * 128, :],
                            ev=f'xE:{c}:{tc}',
                            wait=[f'xBc:{c}:{2 + (tc % 2)}'] +
                            ([f'xEc:{c}:{tc-2}'] if tc >= 2 else []),
                            q=f'qx{tc % 2}')))
                    items.sort(key=lambda t: t[0])
                    for _, f in items:
                        f()

                def ring_win1(c, o):
                    w = []
                    if o >= 2:
                        w.append(f'winoc:{c}:{o-2}')
                    elif c > 0:
                        w.append(f'winoc:{c-1}:{o+14}')
                    if c == 0:
                        w.append(f'cvS:win:{o}')
                    dma(rin[:, o % 2, :, :],
                        win_s[o, :, :].rearrange("p (k m) -> p k m", k=8),
                        ev=f'winL:{c}:{o}', wait=w, q=f'qwin{o % 2}')

                def ring_dt1(c, o):
                    w = []
                    if o >= 2:
                        w.append(f'dtoc:{c}:{o-2}')
                    elif c > 0:
                        w.append(f'dtoc:{c-1}:{o+6}')
                    if c == 0:
                        w.append(f'cvS:wdt:{o}')
                    dma(rdt[:, o % 2, :, :],
                        wdt_s[o, :, :].rearrange("p (k m) -> p k m", k=8),
                        ev=f'dtL:{c}:{o}', wait=w, q=f'qdt{o % 2}')

                def ring_w01(c, o):
                    w = []
                    if o >= 2:
                        w.append(f'w0oc:{c}:{o-2}')
                    elif c > 0:
                        w.append(f'w0oc:{c-1}:{o+30}')
                    if c == 0:
                        w += [f'cvS:k0:{(o // 8) * 8 + t}' for t in (4, 5, 6, 7)]
                    dma(rw0[:, o % 2, :, :],
                        w0_s[o, :, :].rearrange("p (k m) -> p k m", k=8),
                        ev=f'w0L:{c}:{o}', wait=w, q=f'qw0{o % 2}')

                def ring_w21(c, bi):
                    h_, k_ = divmod(bi * 2, 32)
                    w = []
                    if bi >= 2:
                        w.append(f'w2b:{c}:{bi-2}')
                    elif c > 0:
                        w.append(f'w2b:{c-1}:{bi+30}')
                    if c == 0:
                        w.append(f'cvS:k2:{(k_ // 2) * 2}')
                        w.append(f'cvS:k2:{(k_ // 2) * 2 + 1}')
                        if bi < 2:
                            w.append(f'cvS:wout:{4 + bi}')
                    sl = (bi * 2) % 4
                    dma(rw2[:, sl:sl + 2, :],
                        w2_s[h_, :, k_ * 512:(k_ + 2) * 512]
                        .rearrange("p (k o) -> p k o", k=2),
                        ev=f'w2L:{c}:{bi}', wait=w, q=f'qw2{bi % 2}')

                def ring_wout1(c, bi):
                    h_, k_ = divmod(bi * 2, 8)
                    w = []
                    if bi >= 2:
                        w.append(f'wob:{c}:{bi-2}')
                    elif c > 0:
                        w.append(f'wob:{c-1}:{bi+6}')
                    if c == 0:
                        w.append(f'cvS:wout:{bi}')
                        if bi < 2:
                            w.append(f'cvS:wout:{6 + bi}')
                    sl = (bi * 2) % 4
                    dma(rout[:, sl:sl + 2, :],
                        wout_s[h_, :, k_ * 512:(k_ + 2) * 512]
                        .rearrange("p (k o) -> p k o", k=2),
                        ev=f'woL:{c}:{bi}', wait=w, q=f'qwo{bi % 2}')

                for c in range(NCHUNK):
                    chunk_zipper(c)

            @block.tensor
            def _(tensor):
                def mm(bank, lhsT, rhs, start, stop, ev=None, wait=()):
                    b.wts(tensor, wait)
                    ins = tensor.matmul(pm[bank][0:lhsT.shape[-1], 0:rhs.shape[-1]],
                                        lhsT, rhs, start=start, stop=stop)
                    if ev:
                        b.inc(ins, 'pe', 1, ev=ev)

                def bcmm(c, n):
                    bkB = 4 + (n % 2) * 2
                    for kind, bk in ((0, bkB), (1, bkB + 1)):
                        w = [f'mrhs{"B" if kind == 0 else "C"}:{c}:{n}']
                        if c == 0 and n == 0 and kind == 0:
                            w.append('ones32R')
                        if bank_r[bk] is not None:
                            w.append(bank_r[bk])
                        mm(bk, ones32[:, :],
                           mrhs[:, kind, :], True, True,
                           ev=f'bcmm{"B" if kind == 0 else "C"}:{c}:{n}',
                           wait=w)
                        bank_w[bk] = f'bcmm{"B" if kind == 0 else "C"}:{c}:{n}'
                        bank_r[bk] = f'bcev:{c}:{n}'

                for c in range(NCHUNK):
                    for tc in range(NTC):
                        bk = tc % 2
                        waits = [f'xn32:{c}:{tc}']
                        for bb in (bk, bk + 2):
                            if bank_r[bb] is not None:
                                waits.append(bank_r[bb])
                        if c == 0 and tc == 0:
                            waits.append('identR')
                        for h in range(8):
                            b.wts(tensor, waits)
                            waits = []
                            ins = tensor.transpose(
                                pm[bk + (0 if h < 4 else 2)]
                                [:, (h % 4) * 128:(h % 4 + 1) * 128],
                                zb[:, tc % 2, h * 128:(h + 1) * 128],
                                ident[:, :])
                            if h == 7:
                                b.inc(ins, 'pe', 1, ev=f'tpg:{c}:{tc}')
                        bank_w[bk] = bank_w[bk + 2] = f'tpg:{c}:{tc}'
                        bank_r[bk] = bank_r[bk + 2] = f'tpe:{c}:{tc}'

                    for g in range(NG):
                        bk = g % 4
                        waits = [f'winL:{c}:{g}']
                        if g == 0:
                            waits.append(f'tpe:{c}:3')
                        if bank_r[bk] is not None:
                            waits.append(bank_r[bk])
                        for k in range(8):
                            mm(bk, rin[:, g % 2, k, :], fybuf[:, k, :],
                               k == 0, k == 7,
                               ev=[f'ipg:{c}:{g}', f'winoc:{c}:{g}'] if k == 7 else None,
                               wait=waits if k == 0 else ())
                            waits = []
                        bank_w[bk] = f'ipg:{c}:{g}'
                        bank_r[bk] = f'ipe:{c}:{g}'
                    for g in range(NG):
                        o = 8 + g
                        bk = g % 4
                        waits = [f'winL:{c}:{o}']
                        if bank_r[bk] is not None:
                            waits.append(bank_r[bk])
                        for k in range(8):
                            mm(bk, rin[:, o % 2, k, :], fybuf[:, k, :],
                               k == 0, k == 7,
                               ev=[f'gpg:{c}:{g}', f'winoc:{c}:{o}'] if k == 7 else None,
                               wait=waits if k == 0 else ())
                            waits = []
                        bank_w[bk] = f'gpg:{c}:{g}'
                        bank_r[bk] = f'gpe:{c}:{g}'
                    bk = 0
                    waits = [f'xcbd:{c}:{NG-1}']
                    if c == 0:
                        waits.append('ld_rbc')
                    if bank_r[bk] is not None:
                        waits.append(bank_r[bk])
                    for k in range(8):
                        mm(bk, rbc[:, k, :], xcb[:, k, :], k == 0, k == 7,
                           ev=f'bcg:{c}' if k == 7 else None,
                           wait=waits if k == 0 else ())
                        waits = []
                    bank_w[0] = f'bcg:{c}'
                    bank_r[0] = f'bce:{c}'
                    for g in range(NG):
                        bk = 1 + g % 3
                        waits = [f'dtL:{c}:{g}']
                        if bank_r[bk] is not None:
                            waits.append(bank_r[bk])
                        for k in range(8):
                            mm(bk, rdt[:, g % 2, k, :], xcb[:, k, :],
                               k == 0, k == 7,
                               ev=[f'dtg:{c}:{g}', f'dtoc:{c}:{g}'] if k == 7 else None,
                               wait=waits if k == 0 else ())
                            waits = []
                        bank_w[bk] = f'dtg:{c}:{g}'
                        bank_r[bk] = f'dte:{c}:{g}'
                    for n_ in range(4):
                        bcmm(c, n_)
                    for _pos, kind_, idx_ in ffn_schedule():
                        if kind_ == 'bc':
                            bcmm(c, idx_)
                            continue
                        if kind_ == 'f0':
                            o = idx_
                            bk = o % 4
                            waits = [f'w0L:{c}:{o}']
                            if bank_r[bk] is not None:
                                waits.append(bank_r[bk])
                            for k in range(8):
                                mm(bk, rw0[:, o % 2, k, :], fybuf[:, k, :],
                                   k == 0, k == 7,
                                   ev=[f'f0g:{c}:{o}', f'w0oc:{c}:{o}'] if k == 7 else None,
                                   wait=waits if k == 0 else ())
                                waits = []
                            bank_w[bk] = f'f0g:{c}:{o}'
                            bank_r[bk] = f'f0e:{c}:{o}'
                            if o == 31:
                                b.mark('pe', f'ffn0done:{c}')
                            continue
                        o = idx_
                        bk = o % 4
                        for k in range(32):
                            ti = o * 32 + k
                            bi = ti // W1_BATCH
                            w = [f'h1e:{c}:{k}'] if o == 0 else []
                            if ti % W1_BATCH == 0:
                                w.append(f'w1L:{c}:{bi}')
                            if k == 0 and bank_r[bk] is not None:
                                w.append(bank_r[bk])
                            ev = []
                            if k == 31:
                                ev.append(f'f1g:{c}:{o}')
                            if (ti + 1) % W1_BATCH == 0:
                                ev.append(f'w1b:{c}:{bi}')
                            mm(bk, rw1[:, ti % W1_RING, :], h1[:, k, :],
                               k == 0, k == 31, ev=ev or None, wait=w)
                        bank_w[bk] = f'f1g:{c}:{o}'
                        bank_r[bk] = f'f1e:{c}:{o}'
                    b.mark('pe', f'ffn1done:{c}')
                    for h in range(2):
                        for k in range(32):
                            hk = h * 32 + k
                            for tc in range(NTC):
                                bk = 4 + tc
                                w = []
                                if tc == 0:
                                    if h == 0:
                                        w.append(f'h2e:{c}:{k}')
                                    if hk % 2 == 0:
                                        w.append(f'w2L:{c}:{hk // 2}')
                                if k == 0 and bank_r[bk] is not None:
                                    w.append(bank_r[bk])
                                ev = []
                                if k == 31:
                                    ev.append(f'f2g:{c}:{h}:{tc}')
                                if tc == 3 and hk % 2 == 1:
                                    ev.append(f'w2b:{c}:{hk // 2}')
                                mm(bk, h2[:, k, tc * 128:(tc + 1) * 128],
                                   rw2[:, hk % 4, :], k == 0, k == 31,
                                   ev=ev or None, wait=w)
                        for tc in range(NTC):
                            bank_w[4 + tc] = f'f2g:{c}:{h}:{tc}'
                            bank_r[4 + tc] = f'f2e:{c}:{h}:{tc}'
                    b.mark('pe', f'ffn2done:{c}')
                    for h in range(2):
                        for k in range(8):
                            hk = h * 8 + k
                            for tc in range(NTC):
                                bk = tc * 2 + h
                                w = []
                                if tc == 0:
                                    if hk == 0:
                                        w.append(f'ybfall:{c}')
                                    if hk % 2 == 0:
                                        w.append(f'woL:{c}:{hk // 2}')
                                if k == 0 and bank_r[bk] is not None:
                                    w.append(bank_r[bk])
                                ev = []
                                if k == 7:
                                    ev.append(f'opg:{c}:{tc}:{h}')
                                if tc == 3 and hk % 2 == 1:
                                    ev.append(f'wob:{c}:{hk // 2}')
                                mm(bk, fybuf[:, k, tc * 128:(tc + 1) * 128],
                                   rout[:, hk % 4, :], k == 0, k == 7,
                                   ev=ev or None, wait=w)
                    for h in range(2):
                        for tc in range(NTC):
                            bank_w[tc * 2 + h] = f'opg:{c}:{tc}:{h}'
                            bank_r[tc * 2 + h] = f'ope:{c}:{tc}'

            @block.scalar
            def _(scalar):
                def act(out, in_, func, ev=None, wait=(), **kw):
                    b.wts(scalar, wait)
                    ins = scalar.activation(out=out, in_=in_, func=func, **kw)
                    if ev:
                        b.inc(ins, 'act', 1, ev=ev)
                    return ins

                def afence(last_ins):
                    b.inc(last_ins, 'act', 1)
                    scalar.wait_ge(b.sems['act'], b.counts['act'])

                act(apos[:, :], cst[:, OA:OA + 128], AF.Exp, ev='aposR',
                    wait=('ld_small',))

                for c in range(NCHUNK):
                    for tc in range(NTC):
                        w = [f'xB:{c}:{tc}']
                        if tc >= 1:
                            w.append(f'ln1v:{c}:{tc-1}')
                        if tc >= 2:
                            w.append(f'tpg:{c}:{tc-2}')
                        elif c > 0:
                            w.append(f'mst:{c-1}:{tc+2}')
                        act(zb[:, tc % 2, :], xst[:, tc % 2, :], AF.Copy,
                            wait=w, accum_out=stats[:, 0:1])
                        act(zb[:, tc % 2, :], xst[:, tc % 2, :], AF.Square,
                            accum_out=stats[:, 1:2], ev=f'ln1s:{c}:{tc}')
                        ins = act(stats[:, 2:3], stats[:, 2:3], AF.Ln,
                                  wait=(f'ln1v:{c}:{tc}',))
                        afence(ins)
                        act(rs_c[:, tc:tc + 1], stats[:, 2:3], AF.Exp,
                            scale=-0.5, ev=f'ln1r:{c}:{tc}')
                    for g in range(NG):
                        w = [f'ipg:{c}:{g}', f'padc:{c}:{g}']
                        if g >= 2:
                            w.append(f'convd:{c}:{g-2}')
                        elif c > 0:
                            w.append(f'convd:{c-1}:{g+6}')
                        act(xssm[:, g % 2, 3:515], pm[g % 4][:, :], AF.Identity,
                            bias=cst[:, OB + g:OB + g + 1], wait=w, ev=f'ipe:{c}:{g}')
                        w = [f'conv4:{c}:{g}']
                        if c > 0:
                            w += [f'dtg:{c-1}:{NG-1}', f'dtxd:{c-1}:{g}']
                        act(xcb[:, g, :], convacc[:, g % 2, :], AF.Silu,
                            bias=cst[:, OCB + g:OCB + g + 1], wait=w, ev=f'xcbd:{c}:{g}')
                    for g in range(NG):
                        w = [f'gpg:{c}:{g}']
                        if c > 0:
                            w.append(f'ybf:{c-1}:{g}')
                        act(gateb[:, g, :], pm[g % 4][:, :], AF.Silu,
                            bias=cst[:, OB + 8 + g:OB + 9 + g], wait=w, ev=f'gpe:{c}:{g}')
                    w = [f'bcg:{c}']
                    if c == 0:
                        w.append('bccolR')
                    if c >= 2:
                        w.append(f'mrhsd:{c-2}:{2*NN-1}')
                    act(bcst[:, c % 2, :], pm[0][0:32, :], AF.Identity,
                        scale=bcsc[0:32, :], bias=cst[0:32, OBC:OBC + 1],
                        wait=w, ev=f'bce:{c}')
                    for g in range(NG):
                        w = [f'dtg:{c}:{g}']
                        if c == 0 and g == 0:
                            w.append('ndtbR')
                        act(sdt[:, :], pm[1 + g % 3][:, :], AF.Sigmoid,
                            scale=-1.0, bias=cst[:, ONDT + g:ONDT + g + 1], wait=w)
                        w = []
                        if c > 0:
                            w.append(f'dtxd:{c-1}:{g}')
                        act(lbf[:, g, :], sdt[:, :], AF.Ln, wait=w,
                            ev=[f'dte:{c}:{g}', f'lbfd:{c}:{g}'])
                    def bc_evac(n):
                        w = [f'bcmmC:{c}:{n}']
                        if n >= 3:
                            w += [f'bcuseB:{c}:{n-3}', f'bcuseC:{c}:{n-3}']
                        elif c > 0:
                            prev = 15 if n == 0 else 12 + n
                            w += [f'bcuseB:{c-1}:{prev}',
                                  f'bcuseC:{c-1}:{prev}']
                        bk = 4 + (n % 2) * 2
                        act(negb[:, n % 3, :], pm[bk][:, :], AF.Identity,
                            wait=w)
                        act(cbr[:, n % 3, :], pm[bk + 1][:, :], AF.Identity,
                            ev=[f'bcL:{c}:{n}', f'bcev:{c}:{n}'])

                    f0pos = {4 + (o * 3) // 2: o for o in range(32)}
                    f1pos = {56 + (o * 17) // 10: o for o in range(32)}
                    bc_evac(0)
                    bc_evac(1)
                    bc_evac(2)
                    bcpos = {max(1, 8 * n - 6): n for n in range(3, NN)}
                    for i in range(128):
                        n, g = divmod(i, NG)
                        w = []
                        if c == 0 and i == 0:
                            w.append('aposR')
                        if n == 0:
                            w.append(f'lbfd:{c}:{g}')
                        if i >= 2:
                            w.append(f'scank:{c}:{i-2}')
                        elif c > 0:
                            w.append(f'scank:{c-1}:{126+i}')
                        act(da[:, i % 2, :], lbf[:, g, :], AF.Exp,
                            scale=apos[:, g * 16 + n:g * 16 + n + 1],
                            wait=w, ev=f'dA:{c}:{i}')
                        if i in bcpos:
                            bc_evac(bcpos[i])
                        if i in f0pos:
                            o = f0pos[i]
                            w = [f'f0g:{c}:{o}']
                            if c == 0 and o == 0:
                                w.append('scolR')
                            if c > 0:
                                w.append(f'ffn1done:{c-1}')
                            act(h1[:, o, :], pm[o % 4][:, :], AF.Relu,
                                scale=scol[:, 0:1], wait=w,
                                ev=[f'f0e:{c}:{o}', f'h1e:{c}:{o}'])
                        if i in f1pos:
                            o = f1pos[i]
                            w = [f'f1g:{c}:{o}']
                            if c > 0:
                                w.append(f'ffn2done:{c-1}')
                            act(h2[:, o, :], pm[o % 4][:, :], AF.Relu,
                                scale=scol[:, 1:2], wait=w,
                                ev=[f'f1e:{c}:{o}', f'h2e:{c}:{o}'])
                    for h in range(2):
                        for tc in range(NTC):
                            w = [f'f2g:{c}:{h}:{tc}']
                            if h == 0 and c > 0:
                                w.append(f'mcp:{c-1}:{tc}')
                            act(baseb[:, tc * 2 + h, :], pm[4 + tc][:, :],
                                AF.Copy, scale=scol[:, 2:3], wait=w,
                                ev=f'f2e:{c}:{h}:{tc}')
                    for tc in range(NTC):
                        w = [f'zrdy:{c}:{tc}']
                        if tc >= 1:
                            w.append(f'ln2v:{c}:{tc-1}')
                        act(sdt[:, :], zb[:, tc % 2, 0:512], AF.Copy,
                            accum_out=stats[:, 3:4], wait=w)
                        act(sdt[:, :], zb[:, tc % 2, 512:1024], AF.Copy,
                            accum_out=stats[:, 4:5])
                        act(sdt[:, :], zb[:, tc % 2, 0:512], AF.Square,
                            accum_out=stats[:, 5:6])
                        act(sdt[:, :], zb[:, tc % 2, 512:1024], AF.Square,
                            accum_out=stats[:, 6:7], ev=f'ln2s:{c}:{tc}')
                        ins = act(stats[:, 8:9], stats[:, 7:8], AF.Ln,
                                  wait=(f'ln2v:{c}:{tc}',))
                        afence(ins)
                        act(stats[:, 9:10], stats[:, 8:9], AF.Exp, scale=-0.5,
                            ev=f'ln2r:{c}:{tc}')

            @block.vector
            def _(vector):
                vector = EngWrap(vector)

                def w_(*evs):
                    b.wts(vector, [e for e in evs if e])

                def tick(ev):
                    b.inc(vector.last, 'dve', 1, ev=ev)

                def fence():
                    b.inc(vector.last, 'dve', 1)
                    vector.wait_ge(b.sems['dve'], b.counts['dve'])

                w_('ld_small')
                vector.tensor_scalar(out=cst[:, ONDT:ONDT + 8],
                                     in0=cst[:, ONDT:ONDT + 8],
                                     scalar1=-1.0, scalar2=None, op0=ALU.mult)
                tick('ndtbR')
                vector.tensor_scalar(out=scol[:, :], in0=cst[:, OS:OS + 3],
                                     scalar1=2.0, scalar2=None, op0=ALU.mult)
                tick('scolR')
                vector.memset(bcsc[0:32, :], 1.0)
                fence()
                vector.tensor_scalar(out=bcsc[0:16, :], in0=bcsc[0:16, :],
                                     scalar1=-1.0, scalar2=None, op0=ALU.mult)
                vector.tensor_scalar(out=cst[0:16, OBC:OBC + 1],
                                     in0=cst[0:16, OBC:OBC + 1],
                                     scalar1=-1.0, scalar2=None, op0=ALU.mult)
                tick('bccolR')
                w_('ld_wbc')
                vector.tensor_copy(out=rbc[:, :, :],
                                   in_=sdt[:, 0:256]
                                   .rearrange("p (g m) -> p g m", g=8))
                tick('ld_rbc')

                def kconv_range(q0, q1):
                    for q in range(q0, q1):
                        fam, i = KLIST[q]
                        ws = []
                        if q >= 4:
                            pf, pi = KLIST[q - 4]
                            ws.append(f'cvS:{pf}:{pi}')
                        if fam == 'k0':
                            ws += [f'cvL0:k0:{i}', f'cvL1:k0:{i}']
                            w_(*ws)
                            vector.tensor_tensor(
                                out=cb2_slot(q % 4),
                                in0=cu8[:, (2 * i) % 4, :],
                                in1=cu8[:, (2 * i + 1) % 4, :],
                                op=ALU.subtract)
                        else:
                            ws.append(f'cvL:{fam}:{i}')
                            w_(*ws)
                            vector.tensor_scalar(out=cb2_slot(q % 4),
                                                 in0=cu8[:, i % 4, :],
                                                 scalar1=2.0, scalar2=-1.0,
                                                 op0=ALU.mult, op1=ALU.add)
                        tick(f'cvD:{fam}:{i}')

                kconv_range(0, 32)

                for c in range(NCHUNK):
                    for tc in range(NTC):
                        w_(f'ln1s:{c}:{tc}')
                        vector.tensor_scalar(out=mu_c[:, tc:tc + 1],
                                             in0=stats[:, 0:1],
                                             scalar1=1.0 / D, scalar2=None,
                                             op0=ALU.mult)
                        vector.tensor_scalar(out=stats[:, 2:3],
                                             in0=stats[:, 1:2],
                                             scalar1=1.0 / D, scalar2=None,
                                             op0=ALU.mult)
                        fence()
                        vector.tensor_scalar(out=stats[:, 1:2],
                                             in0=mu_c[:, tc:tc + 1],
                                             scalar1=mu_c[:, tc:tc + 1],
                                             scalar2=None, op0=ALU.mult)
                        fence()
                        vector.tensor_sub(out=stats[:, 2:3], in0=stats[:, 2:3],
                                          in1=stats[:, 1:2])
                        fence()
                        vector.tensor_scalar(out=stats[:, 2:3],
                                             in0=stats[:, 2:3],
                                             scalar1=float(EPS), scalar2=None,
                                             op0=ALU.add)
                        tick(f'ln1v:{c}:{tc}')
                        w_(f'ln1r:{c}:{tc}')
                        vector.tensor_scalar(
                            out=zb[:, tc % 2, :], in0=xst[:, tc % 2, :],
                            scalar1=mu_c[:, tc:tc + 1],
                            scalar2=rs_c[:, tc:tc + 1],
                            op0=ALU.subtract, op1=ALU.mult)
                        tick([f'xn32:{c}:{tc}', f'xBc:{c}:{tc}'])
                    for tc in range(NTC):
                        w_(f'tpg:{c}:{tc}',
                           f'opg:{c-1}:3:1' if (c > 0 and tc == 0) else None)
                        for h in range(8):
                            bk = (tc % 2) + (0 if h < 4 else 2)
                            vector.tensor_copy(
                                out=fybuf[:, h, tc * 128:(tc + 1) * 128],
                                in_=pm[bk][:, (h % 4) * 128:(h % 4 + 1) * 128])
                        tick(f'tpe:{c}:{tc}')
                    for g in range(NG):
                        if c % 4 == 0:
                            w_(f'convd:{c-1}:{g+6}' if (c > 0 and g < 2) else None)
                            vector.memset(xssm[:, g % 2, 0:3], 0.0)
                            tick(f'padc:{c}:{g}')
                        else:
                            w_(f'pads:{c-1}:{g}')
                            vector.tensor_copy(out=xssm[:, g % 2, 0:3],
                                               in_=pads[:, g, :])
                            tick(f'padc:{c}:{g}')
                        w_(f'ipe:{c}:{g}')
                        if (c + 1) % 4 != 0:
                            vector.tensor_copy(out=pads[:, g, :],
                                               in_=xssm[:, g % 2, 512:515])
                            tick(f'pads:{c}:{g}')
                        w_(f'xcbd:{c}:{g-2}' if g >= 2 else None,
                           f'padc:{c}:{g}')
                        vector.tensor_scalar(
                            out=convacc[:, g % 2, :],
                            in0=xssm[:, g % 2, 0:512],
                            scalar1=cst[:, OCW + 4 * g:OCW + 4 * g + 1],
                            scalar2=None, op0=ALU.mult)
                        for kk in range(1, 4):
                            vector.scalar_tensor_tensor(
                                out=convacc[:, g % 2, :],
                                in0=xssm[:, g % 2, kk:kk + 512],
                                scalar=cst[:, OCW + 4 * g + kk:OCW + 4 * g + kk + 1],
                                in1=convacc[:, g % 2, :],
                                op0=ALU.mult, op1=ALU.add)
                        tick([f'conv4:{c}:{g}', f'convd:{c}:{g}'])
                        w_(f'xcbd:{c}:{g}',
                           f'yadd:{c-1}:127' if (c > 0 and g == 0) else None)
                        vector.tensor_scalar(out=yacc[:, g, :],
                                             in0=xcb[:, g, :],
                                             scalar1=cst[:, OD + g:OD + g + 1],
                                             scalar2=None, op0=ALU.mult)
                        tick(f'yini:{c}:{g}')
                    for g in range(NG):
                        w_(f'lbfd:{c}:{g}')
                        vector.tensor_mul(out=dtxb[:, g, :], in0=xcb[:, g, :],
                                          in1=lbf[:, g, :])
                        tick(f'dtxd:{c}:{g}')
                    if c == 0:
                        kconv_range(32, len(KLIST))
                    def gbuild(n):
                        for kind, row in ((0, n), (1, 16 + n)):
                            w_(f'bce:{c}' if (n == 0 and kind == 0) else None,
                               f'bcmm{"B" if kind == 0 else "C"}:{c}:{n-1}'
                               if n >= 1 else None)
                            vector.tensor_scalar(
                                out=mrhs[:, kind, :],
                                in0=bcst[:, c % 2, :],
                                scalar1=ident[0:32, row:row + 1],
                                scalar2=None, op0=ALU.mult)
                            evs = [f'mrhs{"B" if kind == 0 else "C"}:{c}:{n}']
                            if n == NN - 1 and kind == 1:
                                evs.append(f'mrhsd:{c}:{2*NN-1}')
                            b.inc(vector.last, 'dve', 1, ev=evs)

                    for n0 in range(4):
                        gbuild(n0)
                    for i in range(128):
                        n, g = divmod(i, NG)
                        if i >= 10 and (i - 10) % 8 == 0 and (i - 10) // 8 + 4 < NN:
                            gbuild((i - 10) // 8 + 4)
                        w_(f'bcL:{c}:{n}' if g == 0 else None,
                           f'dtxd:{c}:{g}' if n == 0 else None)
                        vector.tensor_mul(
                            out=ub[:, :],
                            in0=dtxb[:, g, :], in1=negb[:, n % 3, :])
                        if g == NG - 1:
                            tick(f'bcuseB:{c}:{n}')
                        w_(f'dA:{c}:{i}',
                           f'ymul:{c}:{i-2}' if i >= 2 else
                           (f'ymul:{c-1}:{126+i}' if c > 0 else None),
                           f'carrydone:{c-1}' if (i == 0 and c % 4 != 0) else None,
                           f'hcclear:{c}' if (i == 0 and c % 4 == 0) else None)
                        ins = vector.tensor_tensor_scan(
                            out=hb[:, i % 2, :], data0=da[:, i % 2, :],
                            data1=ub[:, :],
                            initial=hc[:, g * 16 + n:g * 16 + n + 1],
                            op0=ALU.mult, op1=ALU.add)
                        b.inc(ins, 'dve', 1, ev=f'scank:{c}:{i}')
                        if i >= 1:
                            w_(f'ymul:{c}:{i-1}')
                            gp = (i - 1) % NG
                            vector.tensor_add(out=yacc[:, gp, :],
                                              in0=yacc[:, gp, :],
                                              in1=pb[:, (i - 1) % 2, :])
                            tick(f'yadd:{c}:{i-1}')
                    w_(f'ymul:{c}:127')
                    vector.tensor_add(out=yacc[:, 7, :], in0=yacc[:, 7, :],
                                      in1=pb[:, 1, :])
                    tick(f'yadd:{c}:127')
                    for g in range(NG):
                        w_(f'ffn0done:{c}' if g == 0 else None)
                        vector.tensor_mul(out=fybuf[:, g, :],
                                          in0=yacc[:, g, :],
                                          in1=gateb[:, g, :])
                        tick([f'ybf:{c}:{g}'] +
                             ([f'ybfall:{c}'] if g == NG - 1 else []))
                    for tc in range(NTC):
                        w_(f'xE:{c}:{tc}',
                           f'tpg:{c}:{tc+2}' if tc < 2 else None,
                           f'mst:{c}:{tc-2}' if tc >= 2 else None)
                        vector.tensor_scalar(
                            out=zb[:, tc % 2, :], in0=xst[:, tc % 2, :],
                            scalar1=mu_c[:, tc:tc + 1],
                            scalar2=rs_c[:, tc:tc + 1],
                            op0=ALU.subtract, op1=ALU.mult)
                        fence()
                        if c == 0 and tc == 0:
                            w_('ld_reps')
                        for h in range(2):
                            w_(f'opg:{c}:{tc}:{h}')
                            vector.tensor_add(
                                out=zb[:, tc % 2, h * 512:(h + 1) * 512],
                                in0=zb[:, tc % 2, h * 512:(h + 1) * 512],
                                in1=pm[tc * 2 + h][:, :])
                            fence()
                        vector.tensor_add(out=zb[:, tc % 2, :],
                                          in0=zb[:, tc % 2, :],
                                          in1=outbr[:, :])
                        tick([f'zrdy:{c}:{tc}', f'ope:{c}:{tc}'])
                        w_(f'ln2s:{c}:{tc}')
                        vector.tensor_add(out=stats[:, 3:4], in0=stats[:, 3:4],
                                          in1=stats[:, 4:5])
                        vector.tensor_add(out=stats[:, 5:6], in0=stats[:, 5:6],
                                          in1=stats[:, 6:7])
                        fence()
                        vector.tensor_scalar(out=stats[:, 3:4],
                                             in0=stats[:, 3:4],
                                             scalar1=1.0 / D, scalar2=None,
                                             op0=ALU.mult)
                        vector.tensor_scalar(out=stats[:, 5:6],
                                             in0=stats[:, 5:6],
                                             scalar1=1.0 / D, scalar2=None,
                                             op0=ALU.mult)
                        fence()
                        vector.tensor_scalar(out=stats[:, 4:5],
                                             in0=stats[:, 3:4],
                                             scalar1=stats[:, 3:4],
                                             scalar2=None, op0=ALU.mult)
                        fence()
                        vector.tensor_sub(out=stats[:, 7:8], in0=stats[:, 5:6],
                                          in1=stats[:, 4:5])
                        fence()
                        vector.tensor_scalar(out=stats[:, 7:8],
                                             in0=stats[:, 7:8],
                                             scalar1=float(EPS), scalar2=None,
                                             op0=ALU.add)
                        tick(f'ln2v:{c}:{tc}')
                        w_(f'ln2r:{c}:{tc}')
                        vector.tensor_scalar(
                            out=zb[:, tc % 2, :], in0=zb[:, tc % 2, :],
                            scalar1=stats[:, 3:4], scalar2=stats[:, 9:10],
                            op0=ALU.subtract, op1=ALU.mult)
                        fence()
                        vector.tensor_mul(out=zb[:, tc % 2, :],
                                          in0=zb[:, tc % 2, :],
                                          in1=lngr[:, :])
                        fence()
                        vector.tensor_add(out=zb[:, tc % 2, :],
                                          in0=zb[:, tc % 2, :],
                                          in1=lnbr[:, :])
                        fence()
                        vector.tensor_add(out=zb[:, tc % 2, :],
                                          in0=zb[:, tc % 2, :],
                                          in1=xst[:, tc % 2, :])
                        tick([f'zdone:{c}:{tc}', f'xEc:{c}:{tc}'])
                if DEBUG_DUMP and c == NCHUNK - 1:
                    for bk in range(8):
                        if bk > 0:
                            b.wt(vector, f'dpmst:{bk-1}')
                        vector.tensor_copy(out=sdt[:, :], in_=pm[bk][:, :])
                        tick(f'dpm:{bk}')

            @block.gpsimd
            def _(gpsimd):
                gpsimd = EngWrap(gpsimd)

                def w_(*evs):
                    b.wts(gpsimd, [e for e in evs if e])

                def tick(ev):
                    b.inc(gpsimd.last, 'gps', 1, ev=ev)

                gpsimd.memset(ident[:, :], 0.0)
                b.inc(gpsimd.last, 'gps', 1)
                gpsimd.wait_ge(b.sems['gps'], b.counts['gps'])
                gpsimd.affine_select(out=ident[:, :], in_=ident[:, :],
                                     compare_op=ALU.not_equal, fill=1.0,
                                     base=0, pattern=[[-1, 128]],
                                     channel_multiplier=1)
                tick('identR')
                gpsimd.memset(hc[:, :], 0.0)
                tick('hcclear:0')
                gpsimd.memset(ones32[:, :], 1.0)
                tick('ones32R')
                b.inc(gpsimd.dma_start(out=outbr[:, :], in_=outb_h), 'qr', 16)
                b.inc(gpsimd.dma_start(out=lngr[:, :], in_=lng_h), 'qr', 16)
                b.inc(gpsimd.dma_start(out=lnbr[:, :], in_=lnb_h), 'qr', 16)
                b.mark('qr', 'ld_reps')

                qg_pieces = []
                for j in range(16):
                    qg_pieces.append((win_h[:, j, :, :]
                                      .rearrange("p k m -> p (k m)"),
                                      win_s[j, :, :], f'cvS:win:{j}'))
                for j in range(8):
                    qg_pieces.append((wdt_h[:, j, :, :]
                                      .rearrange("p k m -> p (k m)"),
                                      wdt_s[j, :, :], f'cvS:wdt:{j}'))
                for j in range(8):
                    h_, k_ = divmod(2 * j, 8)
                    qg_pieces.append((wout_h
                                      .rearrange("p h k o -> p (h k o)")
                                      [:, j * 1024:(j + 1) * 1024],
                                      wout_s[h_, :, k_ * 512:(k_ + 2) * 512],
                                      f'cvS:wout:{j}'))
                NQG = len(qg_pieces)

                for g in (0, 1):
                    b.inc(gpsimd.dma_start(out=cbf_slot(g % 4),
                                           in_=qg_pieces[g][0]),
                          f'qgl{g % 4}', 16, ev=f'cvLg:{g}')
                for g in range(NQG):
                    if g + 2 < NQG:
                        w_(qg_pieces[g - 2][2] if g >= 2 else None)
                        b.inc(gpsimd.dma_start(out=cbf_slot((g + 2) % 4),
                                               in_=qg_pieces[g + 2][0]),
                              f'qgl{(g + 2) % 4}', 16, ev=f'cvLg:{g+2}')
                    w_(f'cvLg:{g}')
                    b.inc(gpsimd.dma_start(out=qg_pieces[g][1],
                                           in_=cbf_slot(g % 4)),
                          f'qgs{g % 4}', 16, ev=qg_pieces[g][2])

                for c in range(NCHUNK):
                    s, j0 = c // 4, (c % 4)
                    tok0 = j0 * TC
                    if c == 4:
                        w_(f'scank:3:127')
                        gpsimd.memset(hc[:, :], 0.0)
                        tick('hcclear:4')
                    for i in range(128):
                        n, g = divmod(i, NG)
                        w_(f'scank:{c}:{i}',
                           f'yadd:{c}:{i-2}' if i >= 2 else
                           (f'yadd:{c-1}:{126+i}' if c > 0 else None))
                        gpsimd.tensor_mul(out=pb[:, i % 2, :],
                                          in0=hb[:, i % 2, :],
                                          in1=cbr[:, n % 3, :])
                        ins = gpsimd.tensor_copy(
                            out=hc[:, g * 16 + n:g * 16 + n + 1],
                            in_=hb[:, i % 2, 511:512])
                        evs = [f'ymul:{c}:{i}']
                        if g == NG - 1:
                            evs.append(f'bcuseC:{c}:{n}')
                        if i == 127:
                            evs.append(f'carrydone:{c}')
                        b.inc(ins, 'gps', 1, ev=evs)
                    for tc in range(NTC):
                        w_(f'zdone:{c}:{tc}', f'f2e:{c}:{1}:{tc}')
                        gpsimd.tensor_tensor(
                            out=zb[:, tc % 2, :], in0=zb[:, tc % 2, :],
                            in1=baseb[:, tc * 2:tc * 2 + 2, :]
                            .rearrange("p a t -> p (a t)"),
                            op=ALU.add)
                        b.inc(gpsimd.last, 'gps', 1, ev=f'mcp:{c}:{tc}')
                        t0 = tok0 + tc * 128
                        b.inc(gpsimd.dma_start(
                            out=out_hbm[s, t0:t0 + 128, :],
                            in_=zb[:, tc % 2, :]), f'qst{tc % 2}', 16,
                            ev=f'mst:{c}:{tc}')

                if DEBUG_DUMP:
                    for bk in range(8):
                        b.wt(gpsimd, f'dpm:{bk}')
                        b.inc(gpsimd.dma_start(
                            out=dbg["d_pm"][:, bk, :], in_=sdt[:, :]), 'qst0', 16,
                            ev=f'dpmst:{bk}')
                    for nm, sbuf in [("d_xcb", xcb), ("d_lbf", lbf),
                                     ("d_gate", gateb), ("d_fy", fybuf),
                                     ("d_yacc", yacc), ("d_h1", h1),
                                     ("d_h2", h2), ("d_dtx", dtxb),
                                     ("d_zb", zb), ("d_negb", negb),
                                     ("d_outbr", outbr), ("d_lngr", lngr),
                                     ("d_lnbr", lnbr), ("d_base", baseb),
                                     ("d_rout", rout), ("d_hc", hc),
                                     ("d_xst", xst), ("d_rw1", rw1),
                                     ("d_rw0", rw0),
                                     ("d_cbr", cbr), ("d_stats", stats),
                                     ("d_mu", mu_c), ("d_rs", rs_c)]:
                        b.inc(gpsimd.dma_start(out=dbg[nm],
                                               in_=sbuf[...]), 'qst0', 16)
                for s_ in ('qst0', 'qst1'):
                    gpsimd.wait_ge(b.sems[s_], b.counts[s_])

    return nc, b


_CACHE = {}


def _get_program():
    if 'nc' not in _CACHE:
        _, bA = build_program(None)
        nc, _ = build_program(bA.events)
        _CACHE['nc'] = nc
    return _CACHE['nc']


def make_inputs(inputs):
    x = np.ascontiguousarray(np.asarray(inputs['L0_attn_x'], dtype=np.float32))
    ipw = np.asarray(inputs['L0_attn_in_proj_w'], dtype=np.float32)
    ipb = np.asarray(inputs['L0_attn_in_proj_b'], dtype=np.float32)
    cw = np.asarray(inputs['L0_attn_conv_w'], dtype=np.float32)[:, 0, :]
    cb = np.asarray(inputs['L0_attn_conv_b'], dtype=np.float32)
    bcw = np.asarray(inputs['L0_attn_bc_w'], dtype=np.float32)
    bcb = np.asarray(inputs['L0_attn_bc_b'], dtype=np.float32)
    dtw = np.asarray(inputs['L0_attn_dt_w'], dtype=np.float32)
    dtb = np.asarray(inputs['L0_attn_dt_b'], dtype=np.float32)
    alog = np.asarray(inputs['L0_attn_A_log'], dtype=np.float32)
    Dv = np.asarray(inputs['L0_attn_D'], dtype=np.float32)
    ow = np.asarray(inputs['L0_attn_out_w'], dtype=np.float32)
    ob = np.asarray(inputs['L0_attn_out_b'], dtype=np.float32)
    lng = np.asarray(inputs['L0_attn_ln_g'], dtype=np.float32)
    lnb = np.asarray(inputs['L0_attn_ln_b'], dtype=np.float32)
    k0 = np.asarray(inputs['L0_ffn_k0'])
    k1 = np.asarray(inputs['L0_ffn_k1'])
    k2 = np.asarray(inputs['L0_ffn_k2'])
    s0 = np.asarray(inputs['L0_ffn_s0'], dtype=np.float32)
    s1 = np.asarray(inputs['L0_ffn_s1'], dtype=np.float32)
    s2 = np.asarray(inputs['L0_ffn_s2'], dtype=np.float32)

    win_h = np.ascontiguousarray(
        ipw.reshape(16, 128, 8, 128).transpose(3, 0, 2, 1))
    wdt_h = np.ascontiguousarray(
        dtw.reshape(8, 128, 8, 128).transpose(3, 0, 2, 1))
    wout_h = np.ascontiguousarray(
        ow.reshape(2, 512, 8, 128).transpose(3, 0, 2, 1))
    wbc_h = np.ascontiguousarray(bcw.reshape(32, 8, 128).transpose(2, 1, 0))
    k0_h = np.ascontiguousarray(k0.view(np.uint8).reshape(2, 8, 128, 4096))
    k1_h = np.ascontiguousarray(k1.view(np.uint8).reshape(32, 128, 4096))
    k2_h = np.ascontiguousarray(k2.view(np.uint8).reshape(32, 128, 1024))
    cst_h = np.zeros((128, 204), np.float32)
    cst_h[:, 0:16] = ipb.reshape(16, 128).T
    cst_h[:, 16:48] = cw.reshape(8, 128, 4).transpose(1, 0, 2).reshape(128, 32)
    cst_h[:, 48:56] = cb.reshape(8, 128).T
    cst_h[:, 56:64] = dtb.reshape(8, 128).T
    cst_h[0:32, 64] = bcb
    cst_h[:, 65:73] = Dv.reshape(8, 128).T
    cst_h[:, 73:76] = np.stack([s0, s1, s2])[None, :]
    cst_h[:, 76:204] = alog.reshape(8, 128, 16).transpose(1, 0, 2).reshape(128, 128)
    outb_h = np.ascontiguousarray(np.broadcast_to(ob, (128, 1024)))
    lng_h = np.ascontiguousarray(np.broadcast_to(lng, (128, 1024)))
    lnb_h = np.ascontiguousarray(np.broadcast_to(lnb, (128, 1024)))

    shared = dict(win_h=win_h, wdt_h=wdt_h, wout_h=wout_h, wbc_h=wbc_h,
                  k0_h=k0_h, k1_h=k1_h, k2_h=k2_h, cst_h=cst_h,
                  outb_h=outb_h, lng_h=lng_h, lnb_h=lnb_h)
    in_maps = []
    for i in range(NCORE):
        m = dict(shared)
        m['x_in'] = np.ascontiguousarray(x[2 * i:2 * i + 2])
        in_maps.append(m)

    return in_maps


def kernel(**inputs):
    in_maps = make_inputs(inputs)
    nc = _get_program()
    res = run_bass_kernel_spmd(nc, in_maps, core_ids=list(range(NCORE)))
    out = np.concatenate([r['out'] for r in res.results], axis=0)
    return out.reshape(16, 2048, 1024).astype(np.float32)
